# version 02 (final)
# baseline (speedup 1.0000x reference)
import numpy as np
import concourse.bass as bass
import concourse.mybir as mybir
from concourse.bass_utils import run_bass_kernel_spmd
import ml_dtypes
from contextlib import ExitStack, contextmanager

F32 = mybir.dt.float32
BF16 = mybir.dt.bfloat16
AF = mybir.ActivationFunctionType
OP = mybir.AluOpType
AX = mybir.AxisListType

T = 2048
G = 8
D = 1024
PADL = 4
NC = 8

ENGINES = ("tensor", "vector", "scalar", "gpsimd", "sync")

_UN = [0]
def uname(n):
    _UN[0] += 1
    return f"{n}_u{_UN[0]}"


class Instr:
    __slots__ = ("engine", "emit", "deps", "dma", "inc", "val", "idx", "line")
    def __init__(self, engine, emit, dma, idx):
        self.engine = engine; self.emit = emit; self.dma = dma
        self.deps = set(); self.inc = False; self.val = None; self.idx = idx

SERIAL = False

class Prog:
    def __init__(self, strict_same_engine=True):
        self.instrs = []
        self.writer = {}
        self.readers = {}
        self.last = {}
        self.strict = strict_same_engine

    def op(self, engine, emit, reads=(), writes=(), dma=False):
        import sys as _sys
        reads = [_normkey(k) for k in reads]
        writes = [_normkey(k) for k in writes]
        ins = Instr(engine, emit, dma, len(self.instrs))
        ins.line = _sys._getframe(1).f_lineno
        for k in reads:
            w = self.writer.get(k)
            if w is not None:
                ins.deps.add(w)
        for k in writes:
            w = self.writer.get(k)
            if w is not None:
                ins.deps.add(w)
            for r in self.readers.get(k, ()):
                ins.deps.add(r)
        for k in reads:
            self.readers.setdefault(k, []).append(ins)
        for k in writes:
            self.writer[k] = ins
            self.readers[k] = []
        if dma:
            prev = self.last.get((engine, True))
            if prev is not None:
                ins.deps.add(prev)
        self.instrs.append(ins)
        self.last[(engine, dma)] = ins
        return ins

    def barrier(self):
        prev = list(self.last.values())
        self.writer.clear()
        self.readers.clear()
        newlast = {}
        for e in ENGINES:
            ins = Instr(e, lambda eng: eng.nop(), False, len(self.instrs))
            ins.line = -1
            ins.deps.update(prev)
            self.instrs.append(ins)
            newlast[(e, False)] = ins
        self.last = newlast

    def _need_sem(self, d, i):
        if d.engine != i.engine:
            return True
        if d.dma or i.dma:
            return True
        return self.strict

    def finalize_and_emit(self, nc, es):
        import os
        tr = os.environ.get("KTRUNC")
        if tr:
            n = int(tr)
            if n < len(self.instrs):
                keep = set(self.instrs[:n])
                self.instrs = self.instrs[:n]
                print(f"KTRUNC: kept {n} instrs")
        if SERIAL:
            for j in range(1, len(self.instrs)):
                self.instrs[j].deps.add(self.instrs[j - 1])
        for i in self.instrs:
            if i.dma:
                i.inc = True
            for d in i.deps:
                if self._need_sem(d, i):
                    d.inc = True
        sems = {}
        for e in ENGINES:
            sems[(e, False)] = es.enter_context(nc.semaphore(f"s_{e}_c"))
            sems[(e, True)] = es.enter_context(nc.semaphore(f"s_{e}_d"))
        counts = {k: 0 for k in sems}
        for i in self.instrs:
            if i.inc:
                k = (i.engine, i.dma)
                counts[k] += 16 if i.dma else 1
                i.val = counts[k]
        waited = {e: {} for e in ENGINES}
        plan = {e: [] for e in ENGINES}
        for i in self.instrs:
            waits = {}
            for d in i.deps:
                if not self._need_sem(d, i):
                    continue
                k = (d.engine, d.dma)
                if waits.get(k, 0) < d.val:
                    waits[k] = d.val
            w = waited[i.engine]
            real = []
            for k, v in sorted(waits.items(), key=lambda kv: str(kv[0])):
                if w.get(k, 0) < v:
                    w[k] = v
                    real.append((k, v))
            plan[i.engine].append((real, i))
        with nc.Block() as block:
            def mk(name):
                def body(eng):
                    for waits, i in plan[name]:
                        for k, v in waits:
                            eng.wait_ge(sems[k], v)
                        h = i.emit(eng)
                        if i.inc:
                            h.then_inc(sems[(i.engine, i.dma)], 16 if i.dma else 1)
                return body
            block.tensor(mk("tensor"))
            block.vector(mk("vector"))
            block.scalar(mk("scalar"))
            block.gpsimd(mk("gpsimd"))
            block.sync(mk("sync"))

import os as _os
PSLOCK = bool(_os.environ.get("KPSLOCK"))

def K(name, i=None):
    return (name,) if i is None else (name, i)

def _normkey(k):
    if PSLOCK and k and k[0] == "ps":
        return ("ps",)
    return k

def KS(name, idxs):
    return [(name, i) for i in idxs]

XK = KS("x", range(G))
XBK = KS("xb", range(G))

def host_constants():
    c = {}
    c["idn"] = np.eye(128, dtype=ml_dtypes.bfloat16)
    c["idnf"] = np.eye(128, dtype=np.float32)
    c["ones1"] = np.ones((1, 128), dtype=ml_dtypes.bfloat16)
    c["ones1f"] = np.ones((1, 128), dtype=np.float32)
    c["onescol"] = np.ones((128, 1), dtype=ml_dtypes.bfloat16)
    c["selrep16"] = (np.arange(16)[:, None] == (np.arange(16 * 128) // 128)[None, :]).astype(np.float32)
    c["selrep8"] = (np.arange(8)[:, None] == (np.arange(8 * 128) // 128)[None, :]).astype(np.float32)
    pp_ = np.arange(128)[:, None]; uu_ = np.arange(1024)[None, :]
    c["triz"] = (pp_ <= uu_ - 512).astype(ml_dtypes.bfloat16)
    a_ = np.arange(64)
    t64 = (a_[:, None] <= a_[None, :]).astype(ml_dtypes.bfloat16)
    c["triu64"] = np.ascontiguousarray(np.concatenate([t64, t64], axis=0))
    jj = np.arange(4480)[None, :] - 2176 - np.arange(128)[:, None]
    dec = np.where(jj > 0, -0.25 * np.log(np.maximum(jj, 1)) * 8.0, 0.0)
    c["maskL1"] = np.where(jj < 0, -3.0e38, dec).astype(np.float32)
    j0 = np.arange(1536)[None, :] - 512 - np.arange(128)[:, None]
    c["maskL0"] = np.where(j0 < 0, -3.0e38, 0.0).astype(np.float32)
    return c

def prep_weights(inp):
    w = {}
    H, dh = 16, 64
    asn = lambda a: np.ascontiguousarray(np.asarray(a))
    w["l0_wqkv"] = asn(inp["L0_attn_qkv_w"])
    b = np.asarray(inp["L0_attn_qkv_b"])
    w["l0_bqk"] = asn(b[:, :128].reshape(H, 2, dh).transpose(2, 0, 1).reshape(dh, 2 * H))
    bv = b[:, 128:]
    w["l0_bv"] = asn(bv.reshape(G, 2, dh).transpose(1, 2, 0).reshape(128, G))
    w["l0_wo"] = asn(np.asarray(inp["L0_attn_out_w"]).reshape(1024, 1024))
    w["l0_bo"] = asn(np.asarray(inp["L0_attn_out_b"]).reshape(G, 128).T)
    qkv = np.stack([np.asarray(inp["L1_attn_Wq"]).reshape(1024, H, dh),
                    np.asarray(inp["L1_attn_Wk"]).reshape(1024, H, dh),
                    np.asarray(inp["L1_attn_Wv"]).reshape(1024, H, dh)], axis=2)
    w["l1_wqkv"] = asn(qkv.reshape(1024, H, 3 * dh))
    bqk = np.stack([np.asarray(inp["L1_attn_bq"]).reshape(H, dh),
                    np.asarray(inp["L1_attn_bk"]).reshape(H, dh)], axis=1)
    w["l1_bqk"] = asn(bqk.transpose(2, 0, 1).reshape(dh, 2 * H))
    bv1 = np.asarray(inp["L1_attn_bv"]).reshape(H, dh)
    w["l1_bv"] = asn(bv1.reshape(G, 2, dh).transpose(1, 2, 0).reshape(128, G))
    w["l1_wo"] = asn(inp["L1_attn_Wo"])
    w["l1_bo"] = asn(np.asarray(inp["L1_attn_bo"]).reshape(G, 128).T)
    w["l1f_wr"] = asn(np.asarray(inp["L1_ffn_W_r"]).T)
    w["l1f_wk"] = asn(np.asarray(inp["L1_ffn_W_k"]).T)
    w["l1f_wv"] = asn(np.asarray(inp["L1_ffn_W_v"]).T)
    w["l1f_mu"] = asn(np.stack([np.asarray(inp["L1_ffn_mu_r"]).reshape(G, 128).T,
                                np.asarray(inp["L1_ffn_mu_k"]).reshape(G, 128).T], axis=2).reshape(128, 2 * G))
    w["l0f_rw"] = asn(inp["L0_ffn_router_w"])
    w["l0f_rb"] = asn(np.asarray(inp["L0_ffn_router_bias"]).reshape(16, 1))
    w["l0f_gate"] = asn(inp["L0_ffn_expert_gate_w"])
    w["l0f_up"] = asn(inp["L0_ffn_expert_up_w"])
    w["l0f_down"] = asn(inp["L0_ffn_expert_down_w"])
    w["l0f_conv"] = asn(np.asarray(inp["L0_ffn_conv_w"]).transpose(1, 2, 0))
    w["l0f_sup"] = asn(inp["L0_ffn_shared_up_w"])
    w["l0f_sdn"] = asn(inp["L0_ffn_shared_down_w"])
    w["l3f_qkvw"] = asn(np.asarray(inp["L3_ffn_qkv_weight"]).T)
    w["l3f_projw"] = asn(np.asarray(inp["L3_ffn_proj_weight"]).T)
    w["l3f_rw"] = asn(inp["L3_ffn_router_weight"])
    w["l3f_w1"] = asn(inp["L3_ffn_moe_w1"])
    w["l3f_w2"] = asn(inp["L3_ffn_moe_w2"])
    w["l3f_ln"] = asn(np.stack([np.asarray(inp["L3_ffn_ln1_weight"]).reshape(G, 128).T,
                                np.asarray(inp["L3_ffn_ln1_bias"]).reshape(G, 128).T,
                                np.asarray(inp["L3_ffn_ln2_weight"]).reshape(G, 128).T,
                                np.asarray(inp["L3_ffn_ln2_bias"]).reshape(G, 128).T],
                               axis=2).reshape(128, G * 4))
    w["l3a_wdq"] = asn(inp["L3_attn_W_DQ"])
    w["l3a_wdkv"] = asn(inp["L3_attn_W_DKV"])
    w["l3a_wuq"] = asn(inp["L3_attn_W_UQ"])
    w["l3a_wuk"] = asn(inp["L3_attn_W_UK"])
    w["l3a_wuv"] = asn(inp["L3_attn_W_UV"])
    w["l3a_wqr"] = asn(inp["L3_attn_W_QR"])
    w["l3a_wkr"] = asn(inp["L3_attn_W_KR"])
    w["l3a_wo"] = asn(inp["L3_attn_W_O"])
    w["l3a_bd"] = asn(np.stack([np.asarray(inp["L3_attn_b_DQ"]),
                                np.asarray(inp["L3_attn_b_DKV"])], axis=1))
    w["l3a_buqk"] = asn(np.concatenate([np.asarray(inp["L3_attn_b_UQ"]).reshape(G, 128).T,
                                        np.asarray(inp["L3_attn_b_UK"]).reshape(G, 128).T], axis=1))
    w["l3a_bqr"] = asn(np.asarray(inp["L3_attn_b_QR"]).reshape(G, 32).T)
    w["l3a_bkr"] = asn(np.asarray(inp["L3_attn_b_KR"]).reshape(32, 1))
    w["l3a_bo"] = asn(np.asarray(inp["L3_attn_b_O"]).reshape(G, 128).T)
    w["l3a_bvp"] = asn(np.asarray(inp["L3_attn_b_UV"]).reshape(G, 128).T)
    w["l2a_wq"] = asn(np.asarray(inp["L2_attn_Wq"]).T)
    w["l2a_wk"] = asn(np.asarray(inp["L2_attn_Wk"]).T)
    w["l2a_wv"] = asn(np.asarray(inp["L2_attn_Wv"]).T)
    w["l2a_wo"] = asn(np.asarray(inp["L2_attn_Wo"]).T)
    w["l2a_nw"] = asn(np.asarray(inp["L2_attn_norm_w"]).reshape(128, 1))
    w["l2f_rw"] = asn(inp["L2_ffn_router_w"])
    w["l2f_rb"] = asn(np.asarray(inp["L2_ffn_router_b"]).reshape(8, 1))
    w["l2f_w1"] = asn(inp["L2_ffn_w1"])
    b1 = np.asarray(inp["L2_ffn_b1"])
    w["l2f_b1"] = asn(b1.reshape(8, 2, 32, 128).transpose(3, 0, 1, 2).reshape(128, 8 * 64))
    w["l2f_w2"] = asn(inp["L2_ffn_w2"])
    w["l2f_b2"] = asn(np.asarray(inp["L2_ffn_b2"]).reshape(8, G, 128).transpose(2, 0, 1).reshape(128, 8 * G))
    return w


class Ctx:
    pass

@contextmanager
def phase(p, es_outer):
    es = ExitStack()
    try:
        yield es
    finally:
        es.close()
        p.barrier()

def emit_load_consts(p, ctx, dr):
    def ld(buf, name):
        p.op("sync", lambda e, buf=buf, name=name: e.dma_start(buf[:], dr[name][:]),
             reads=[], writes=[K(name)], dma=True)
    ld(ctx.idn, "idn"); ld(ctx.idnf, "idnf")
    ld(ctx.ones1, "ones1"); ld(ctx.ones1f, "ones1f"); ld(ctx.onescol, "onescol")
    p.op("vector", lambda e: e.memset(ctx.xb.ap()[:, :, 0:PADL], 0.0),
         reads=[], writes=XBK)
    p.op("vector", lambda e: e.memset(ctx.xb.ap()[:, :, PADL + T:], 0.0),
         reads=[], writes=XBK)

def emit_load_x(p, ctx, buf0, buf1, x_in):
    for tt in range(16):
        buf = buf0 if tt % 2 == 0 else buf1
        kst = ("ldxb", tt % 2)
        p.op("sync", lambda e, buf=buf, tt=tt: e.dma_start(
            buf[:], x_in[tt * 128:(tt + 1) * 128, :]),
            reads=[kst], writes=[kst], dma=True)
        for half in range(2):
            pb = ctx.psum[half]
            pk = ("ps", half)
            for g4 in range(4):
                g = half * 4 + g4
                p.op("tensor", lambda e, buf=buf, g=g, g4=g4, pb=pb: e.transpose(
                    pb[:, g4 * 128:(g4 + 1) * 128], buf[:, g * 128:(g + 1) * 128], ctx.idnf[:]),
                    reads=[kst, K("idnf")], writes=[pk])
            xv = ctx.x.ap()[:, half * 4:(half + 1) * 4, tt * 128:(tt + 1) * 128]
            xbv = ctx.xb.ap()[:, half * 4:(half + 1) * 4, PADL + tt * 128:PADL + (tt + 1) * 128]
            p.op("vector", lambda e, pb=pb, xv=xv: e.tensor_copy(xv, pb[:]),
                 reads=[pk], writes=KS("x", range(half * 4, half * 4 + 4)))
            p.op("vector", lambda e, pb=pb, xbv=xbv: e.tensor_copy(xbv, pb[:]),
                 reads=[pk], writes=KS("xb", range(half * 4, half * 4 + 4)))

def emit_store_x(p, ctx, buf0, buf1, x_out):
    for tt in range(16):
        buf = buf0 if tt % 2 == 0 else buf1
        kst = ("stxb", tt % 2)
        for half in range(2):
            pb = ctx.psum[half]
            pk = ("ps", half)
            for g4 in range(4):
                g = half * 4 + g4
                xg = ctx.x.ap()[:, g, tt * 128:(tt + 1) * 128]
                p.op("tensor", lambda e, g4=g4, pb=pb, xg=xg: e.transpose(
                    pb[:, g4 * 128:(g4 + 1) * 128], xg, ctx.idnf[:]),
                    reads=[("x", g), K("idnf"), kst], writes=[pk])
            p.op("vector", lambda e, pb=pb, buf=buf, half=half: e.tensor_copy(
                buf[:, half * 512:(half + 1) * 512], pb[:]),
                reads=[pk], writes=[kst])
        p.op("sync", lambda e, buf=buf, tt=tt: e.dma_start(
            x_out[tt * 128:(tt + 1) * 128, :], buf[:]),
            reads=[kst], writes=[kst, ("stout", tt)], dma=True)
    p.op("sync", lambda e: e.nop(), reads=[("stout", tt) for tt in range(16)], writes=[])

def emit_postln(p, ctx, es):
    nc = ctx.nc
    sq0 = es.enter_context(nc.sbuf_tensor(uname("sq0"), [128, 512], BF16, side="right"))
    sq1 = es.enter_context(nc.sbuf_tensor(uname("sq1"), [128, 512], BF16, side="right"))
    rA = es.enter_context(nc.sbuf_tensor(uname("rA"), [1, T], F32, side="right"))
    rB = es.enter_context(nc.sbuf_tensor(uname("rB"), [1, T], F32, side="right"))
    rC = es.enter_context(nc.sbuf_tensor(uname("rC"), [1, T], F32, side="right"))
    epsc = es.enter_context(nc.sbuf_tensor(uname("epsc"), [1, 1], F32, side="right"))
    p.op("vector", lambda e: e.memset(epsc.ap()[:], 1e-5), reads=[], writes=[K("epsc")])
    for g in range(G):
        xg = ctx.x.ap()[:, g, :]
        p.op("scalar", lambda e, g=g, xg=xg: e.copy(
            ctx.xb.ap()[:, g, PADL:PADL + T], xg),
            reads=[("x", g)], writes=[("xb", g)])
    for tc in range(4):
        for g in range(G):
            p.op("tensor", lambda e, tc=tc, g=g: e.matmul(
                ctx.psum[4][0:1, 0:512], ctx.onescol[:],
                ctx.xb.ap()[:, g, PADL + tc * 512:PADL + (tc + 1) * 512],
                start=(g == 0), stop=(g == G - 1)),
                reads=[("xb", g), K("onescol")], writes=[K("ps", 4)])
        p.op("vector", lambda e, tc=tc: e.tensor_scalar(
            rA.ap()[:, tc * 512:(tc + 1) * 512], ctx.psum[4][0:1, 0:512],
            1.0 / D, None, OP.mult),
            reads=[K("ps", 4)], writes=[("rA", tc)])
        for g in range(G):
            sq = sq0 if g % 2 == 0 else sq1
            sk = ("sq", g % 2)
            p.op("vector", lambda e, g=g, tc=tc, sq=sq: e.tensor_tensor(
                sq[:], ctx.xb.ap()[:, g, PADL + tc * 512:PADL + (tc + 1) * 512],
                ctx.xb.ap()[:, g, PADL + tc * 512:PADL + (tc + 1) * 512], OP.mult),
                reads=[("xb", g), sk], writes=[sk])
            p.op("tensor", lambda e, g=g, sq=sq: e.matmul(
                ctx.psum[5][0:1, 0:512], ctx.onescol[:], sq[:],
                start=(g == 0), stop=(g == G - 1)),
                reads=[sk, K("onescol")], writes=[K("ps", 5)])
        p.op("vector", lambda e, tc=tc: e.tensor_scalar(
            rB.ap()[:, tc * 512:(tc + 1) * 512], ctx.psum[5][0:1, 0:512],
            1.0 / D, None, OP.mult),
            reads=[K("ps", 5)], writes=[("rB", tc)])
    RA = KS("rA", range(4)); RB = KS("rB", range(4))
    p.op("vector", lambda e: e.tensor_tensor(rC.ap()[:], rA.ap()[:], rA.ap()[:], OP.mult),
         reads=RA, writes=[K("rC")])
    p.op("vector", lambda e: e.tensor_tensor(rB.ap()[:], rB.ap()[:], rC.ap()[:], OP.subtract),
         reads=RB + [K("rC")], writes=[K("var")])
    p.op("scalar", lambda e: e.activation(rB.ap()[:], rB.ap()[:], AF.Sqrt, bias=epsc.ap()[:], scale=1.0),
         reads=[K("var"), K("epsc")], writes=[K("sq")])
    p.op("vector", lambda e: e.reciprocal(rB.ap()[:], rB.ap()[:]),
         reads=[K("sq")], writes=[K("rstd")])
    p.op("vector", lambda e: e.tensor_tensor(rC.ap()[:], rA.ap()[:], rB.ap()[:], OP.mult),
         reads=RA + [K("rstd")], writes=[K("mr")])
    for tc in range(4):
        p.op("tensor", lambda e, tc=tc: e.matmul(
            ctx.psum[4][:, 0:512], ctx.ones1f[:], rB.ap()[:, tc * 512:(tc + 1) * 512],
            start=True, stop=True),
            reads=[K("rstd"), K("ones1f")], writes=[K("ps", 4)])
        p.op("tensor", lambda e, tc=tc: e.matmul(
            ctx.psum[5][:, 0:512], ctx.ones1f[:], rC.ap()[:, tc * 512:(tc + 1) * 512],
            start=True, stop=True),
            reads=[K("mr"), K("ones1f")], writes=[K("ps", 5)])
        for g in range(G):
            xs = ctx.x.ap()[:, g, tc * 512:(tc + 1) * 512]
            p.op("vector", lambda e, xs=xs, g=g: e.tensor_tensor(
                xs, xs, ctx.psum[4][:, 0:512], OP.mult),
                reads=[("x", g), ("ps", 4)], writes=[("x", g)])
            p.op("vector", lambda e, xs=xs, g=g: e.tensor_tensor(
                xs, xs, ctx.psum[5][:, 0:512], OP.subtract),
                reads=[("x", g), ("ps", 5)], writes=[("x", g)])
            p.op("scalar", lambda e, xs=xs, g=g, tc=tc: e.copy(
                ctx.xb.ap()[:, g, PADL + tc * 512:PADL + (tc + 1) * 512], xs),
                reads=[("x", g)], writes=[("xb", g)])

def emit_attn_16h(p, ctx, es, dr, layer):
    nc = ctx.nc
    H, dh = 16, 64
    scale = 1.0 / np.sqrt(dh)
    CSHIFT = 10.0
    qq = es.enter_context(nc.sbuf_tensor(uname("qq"), [64, T], BF16, side="right"))
    kk = es.enter_context(nc.sbuf_tensor(uname("kk"), [64, T], BF16, side="right"))
    vv = es.enter_context(nc.sbuf_tensor(uname("vv"), [128, 16, 66], BF16, side="right"))
    expS = es.enter_context(nc.sbuf_tensor(uname("expS"), [128, 16, 512], BF16, side="right"))
    attn = es.enter_context(nc.sbuf_tensor(uname("attn"), [128, G, T], BF16, side="right"))
    wo = es.enter_context(nc.sbuf_tensor(uname("wo"), [128, G, 1024], BF16, side="right"))
    wq0 = es.enter_context(nc.sbuf_tensor(uname("wq0"), [128, G, 192], BF16, side="right"))
    wq1 = es.enter_context(nc.sbuf_tensor(uname("wq1"), [128, G, 192], BF16, side="right"))
    bqk = es.enter_context(nc.sbuf_tensor(uname("bqk"), [64, 2 * H], F32, side="right"))
    bvp = es.enter_context(nc.sbuf_tensor(uname("bvp"), [128, G], BF16, side="right"))
    beff = es.enter_context(nc.sbuf_tensor(uname("beff"), [128, G], F32, side="right"))
    arow = es.enter_context(nc.sbuf_tensor(uname("arow"), [64, 512], F32, side="right"))
    rden = es.enter_context(nc.sbuf_tensor(uname("rden"), [1, 512], F32, side="right"))
    anrm = es.enter_context(nc.sbuf_tensor(uname("anrm"), [64, 512], BF16, side="right"))
    csh = es.enter_context(nc.sbuf_tensor(uname("csh"), [128, 1], F32, side="right"))
    if layer == "l1":
        mask = es.enter_context(nc.sbuf_tensor(uname("sb_maskL1"), [128, 4480], F32, side="right"))
        p.op("sync", lambda e: e.dma_start(mask[:], dr["maskL1"][:]),
             reads=[], writes=[K("mask")], dma=True)
        moff = lambda qc, kt: (qc * 512 - kt * 128) + 2176
    else:
        mask = es.enter_context(nc.sbuf_tensor(uname("sb_maskL0"), [128, 1536], F32, side="right"))
        p.op("sync", lambda e: e.dma_start(mask[:], dr["maskL0"][:]),
             reads=[], writes=[K("mask")], dma=True)
        moff = lambda qc, kt: (qc * 512 - kt * 128) + 512
    wqkv = dr[f"{layer}_wqkv"].rearrange("(c p) h d -> p c h d", p=128)
    p.op("sync", lambda e: e.dma_start(bqk[:], dr[f"{layer}_bqk"][:]),
         reads=[], writes=[K("bqk")], dma=True)
    p.op("gpsimd", lambda e: e.dma_start(bvp[:], dr[f"{layer}_bv"][:]),
         reads=[], writes=[K("bvp")], dma=True)
    p.op("gpsimd", lambda e: e.dma_start(
        ctx.wo if False else wo.ap()[:, :, :], dr[f"{layer}_wo"].rearrange("(c p) n -> p c n", p=128)),
        reads=[], writes=[K("wo")], dma=True)
    p.op("sync", lambda e: e.dma_start(beff[:], dr[f"{layer}_bo"][:]),
         reads=[], writes=[K("beff")], dma=True)
    p.op("vector", lambda e: e.memset(vv.ap()[:, :, 64:65], 1.0),
         reads=[], writes=[K("vvones")])
    p.op("vector", lambda e: e.memset(csh.ap()[:], -CSHIFT),
         reads=[], writes=[K("csh")])
    for ge in range(G):
        for g in range(G):
            p.op("tensor", lambda e, g=g, ge=ge: e.matmul(
                ctx.psum[6][:, 0:1], wo.ap()[:, g, ge * 128:(ge + 1) * 128],
                bvp.ap()[:, g:g + 1], start=(g == 0), stop=(g == G - 1)),
                reads=[K("wo"), K("bvp")], writes=[K("ps", 6)])
        p.op("vector", lambda e, ge=ge: e.tensor_tensor(
            beff.ap()[:, ge:ge + 1], beff.ap()[:, ge:ge + 1], ctx.psum[6][:, 0:1], OP.add),
            reads=[K("ps", 6)[0] if False else ("ps", 6), K("beff")], writes=[K("beff"), ("beffd", ge)])
    for h in range(H):
        wbuf = wq0 if h % 2 == 0 else wq1
        wk = ("wq", h % 2)
        p.op("gpsimd", lambda e, h=h, wbuf=wbuf: e.dma_start(
            wbuf.ap()[:, :, :], wqkv[:, :, h, :]),
            reads=[wk], writes=[wk], dma=True)
        for tc in range(4):
            for part, (off, dstbuf, dk) in enumerate(((0, qq, ("qq",)), (64, kk, ("kk",)))):
                pb = ctx.psum[part]
                for g in range(G):
                    p.op("tensor", lambda e, g=g, tc=tc, off=off, pb=pb, wbuf=wbuf: e.matmul(
                        pb[0:64, 0:512], wbuf.ap()[:, g, off:off + 64],
                        ctx.xb.ap()[:, g, PADL + tc * 512:PADL + (tc + 1) * 512],
                        start=(g == 0), stop=(g == G - 1)),
                        reads=[wk, ("xb", g)], writes=[("ps", part)])
                bcol = bqk.ap()[0:64, 2 * h + part:2 * h + part + 1]
                p.op("scalar", lambda e, tc=tc, pb=pb, dstbuf=dstbuf, bcol=bcol: e.activation(
                    dstbuf.ap()[0:64, tc * 512:(tc + 1) * 512], pb[0:64, 0:512],
                    AF.Identity, bias=bcol, scale=1.0),
                    reads=[("ps", part), K("bqk")], writes=[dk])
        for vt in range(16):
            pb = ctx.psum[2]
            for g in range(G):
                p.op("tensor", lambda e, g=g, vt=vt, pb=pb, wbuf=wbuf: e.matmul(
                    pb[:, 0:64], ctx.xb.ap()[:, g, PADL + vt * 128:PADL + (vt + 1) * 128],
                    wbuf.ap()[:, g, 128:192],
                    start=(g == 0), stop=(g == G - 1)),
                    reads=[wk, ("xb", g)], writes=[K("ps", 2)])
            p.op("scalar", lambda e, vt=vt, pb=pb: e.copy(
                vv.ap()[:, vt, 0:64], pb[:, 0:64]),
                reads=[K("ps", 2)], writes=[("vv", vt)])
        gq = h // 2
        odd = h % 2 == 1
        for qc in range(4):
            ktmax = 4 * (qc + 1)
            for kt in range(ktmax):
                pb = ctx.psum[3 + (kt % 2)]
                pk = ("ps", 3 + (kt % 2))
                p.op("tensor", lambda e, kt=kt, qc=qc, pb=pb: e.matmul(
                    pb[:, 0:512], kk.ap()[0:64, kt * 128:(kt + 1) * 128],
                    qq.ap()[0:64, qc * 512:(qc + 1) * 512], start=True, stop=True),
                    reads=[("qq",), ("kk",)], writes=[pk])
                if layer == "l1" or kt >= 4 * qc:
                    mo = moff(qc, kt)
                    p.op("vector", lambda e, pb=pb, mo=mo: e.tensor_tensor(
                        pb[:, 0:512], pb[:, 0:512], mask.ap()[:, mo:mo + 512], OP.add),
                        reads=[pk, K("mask")], writes=[pk])
                p.op("scalar", lambda e, kt=kt, pb=pb: e.activation(
                    ctx.expS if False else expS.ap()[:, kt, :], pb[:, 0:512],
                    AF.Exp, bias=csh.ap()[:], scale=scale),
                    reads=[pk, K("csh")], writes=[("expS", kt)])
            pa = ctx.psum[5]
            for kt in range(ktmax):
                p.op("tensor", lambda e, kt=kt, pa=pa, ktmax=ktmax: e.matmul(
                    pa[0:65, 0:512], vv.ap()[:, kt, 0:65], expS.ap()[:, kt, :],
                    start=(kt == 0), stop=(kt == ktmax - 1)),
                    reads=[("vv", kt), ("expS", kt), K("vvones")], writes=[K("ps", 5)])
            p.op("vector", lambda e, pa=pa: e.reciprocal(rden.ap()[:], pa[64:65, 0:512]),
                 reads=[K("ps", 5)], writes=[K("rden")])
            p.op("scalar", lambda e, pa=pa: e.copy(arow.ap()[:], pa[0:64, 0:512]),
                 reads=[K("ps", 5)], writes=[K("arow")])
            p.op("tensor", lambda e: e.matmul(
                ctx.psum[6][0:64, 0:512], ctx.ones1f[0:1, 0:64], rden.ap()[:],
                start=True, stop=True),
                reads=[K("rden"), K("ones1f")], writes=[K("ps", 6)])
            if not odd:
                dst = attn.ap()[0:64, gq, qc * 512:(qc + 1) * 512]
                p.op("vector", lambda e, dst=dst, gq=gq: e.tensor_tensor(
                    dst, arow.ap()[:], ctx.psum[6][0:64, 0:512], OP.mult),
                    reads=[K("arow"), ("ps", 6)], writes=[("attn", 2 * gq)])
            else:
                p.op("vector", lambda e: e.tensor_tensor(
                    anrm.ap()[:], arow.ap()[:], ctx.psum[6][0:64, 0:512], OP.mult),
                    reads=[K("arow"), ("ps", 6)], writes=[K("anrm")])
                p.op("tensor", lambda e: e.matmul(
                    ctx.psum[6][64:128, 0:512], ctx.idn[0:64, 0:64], anrm.ap()[:],
                    start=True, stop=True, tile_position=(0, 64)),
                    reads=[K("anrm"), K("idn")], writes=[K("ps", 6)])
                dst = attn.ap()[64:128, gq, qc * 512:(qc + 1) * 512]
                p.op("vector", lambda e, dst=dst, gq=gq: e.tensor_copy(
                    dst, ctx.psum[6][64:128, 0:512]),
                    reads=[K("ps", 6)], writes=[("attn", 2 * gq + 1)])
    for ge in range(G):
        for tc in range(4):
            pb = ctx.psum[3 + (tc % 2)]
            pk = ("ps", 3 + (tc % 2))
            for g in range(G):
                p.op("tensor", lambda e, g=g, ge=ge, tc=tc, pb=pb: e.matmul(
                    pb[:, 0:512], wo.ap()[:, g, ge * 128:(ge + 1) * 128],
                    attn.ap()[:, g, tc * 512:(tc + 1) * 512],
                    start=(g == 0), stop=(g == G - 1)),
                    reads=[K("wo"), ("attn", 2 * g), ("attn", 2 * g + 1)], writes=[pk])
            xs = ctx.x.ap()[:, ge, tc * 512:(tc + 1) * 512]
            p.op("vector", lambda e, xs=xs, ge=ge, pb=pb: e.scalar_tensor_tensor(
                xs, pb[:, 0:512], beff.ap()[:, ge:ge + 1], xs, OP.add, OP.add),
                reads=[pk, ("beffd", ge), ("x", ge)], writes=[("x", ge)])

def emit_l0_ffn(p, ctx, es, dr):
    nc = ctx.nc
    E = 16
    rwb = es.enter_context(nc.sbuf_tensor(uname("rwb"), [128, G, 16], BF16, side="right"))
    selrep = es.enter_context(nc.sbuf_tensor(uname("selrep"), [16, 16 * 128], F32, side="right"))
    p.op("sync", lambda e: e.dma_start(selrep[:], dr["selrep16"][:]), reads=[], writes=[K("selrep16")], dma=True)
    rbias = es.enter_context(nc.sbuf_tensor(uname("rbias"), [16, 1], F32, side="right"))
    lfm = es.enter_context(nc.sbuf_tensor(uname("lfm"), [16, T], F32, side="right"))
    ltm = es.enter_context(nc.sbuf_tensor(uname("ltm"), [128, 16, 16], F32, side="right"))
    ctm = es.enter_context(nc.sbuf_tensor(uname("ctm"), [128, 16, 16], F32, side="right"))
    cfm = es.enter_context(nc.sbuf_tensor(uname("cfm"), [16, T], F32, side="right"))
    r1 = es.enter_context(nc.sbuf_tensor(uname("r1"), [128, 12], F32, side="right"))
    crep = es.enter_context(nc.sbuf_tensor(uname("crep"), [128, T], F32, side="right"))
    act = es.enter_context(nc.sbuf_tensor(uname("act"), [128, 4, T], BF16, side="right"))
    sil = es.enter_context(nc.sbuf_tensor(uname("sil"), [128, 512], BF16, side="right"))
    tu = es.enter_context(nc.sbuf_tensor(uname("tu"), [128, 512], BF16, side="right"))
    gw = [es.enter_context(nc.sbuf_tensor(uname(f"gw{i}"), [128, G, 512], BF16, side="right")) for i in range(2)]
    uw = [es.enter_context(nc.sbuf_tensor(uname(f"uw{i}"), [128, G, 512], BF16, side="right")) for i in range(2)]
    dw = [es.enter_context(nc.sbuf_tensor(uname(f"dw{i}"), [128, 4, 1024], BF16, side="right")) for i in range(2)]
    gat = dr["l0f_gate"].rearrange("e (g p) f -> e p g f", p=128)
    upd = dr["l0f_up"].rearrange("e (g p) f -> e p g f", p=128)
    dnd = dr["l0f_down"].rearrange("e (g p) f -> e p g f", p=128)
    p.op("gpsimd", lambda e: e.dma_start(rwb.ap()[:, :, :], dr["l0f_rw"].rearrange("(g p) n -> p g n", p=128)),
         reads=[], writes=[K("rwb")], dma=True)
    p.op("sync", lambda e: e.dma_start(rbias[:], dr["l0f_rb"][:]), reads=[], writes=[K("rbias")], dma=True)
    for tc in range(4):
        pb = ctx.psum[3]
        for g in range(G):
            p.op("tensor", lambda e, g=g, tc=tc, pb=pb: e.matmul(
                pb[0:16, 0:512], rwb.ap()[:, g, :],
                ctx.xb.ap()[:, g, PADL + tc * 512:PADL + (tc + 1) * 512],
                start=(g == 0), stop=(g == G - 1)),
                reads=[K("rwb"), ("xb", g)], writes=[K("ps", 3)])
        p.op("scalar", lambda e, tc=tc, pb=pb: e.activation(
            lfm.ap()[0:16, tc * 512:(tc + 1) * 512], pb[0:16, 0:512],
            AF.Identity, bias=rbias.ap()[0:16, 0:1], scale=1.0),
            reads=[K("ps", 3), K("rbias")], writes=[("lfm", tc)])
    for tt in range(16):
        pb = ctx.psum[4]
        p.op("tensor", lambda e, tt=tt, pb=pb: e.transpose(
            pb[:, 0:16], lfm.ap()[0:16, tt * 128:(tt + 1) * 128], ctx.idnf[0:16, 0:16]),
            reads=[("lfm", tt // 4), K("idnf")], writes=[K("ps", 4)])
        p.op("vector", lambda e, tt=tt, pb=pb: e.tensor_copy(ltm.ap()[:, tt, :], pb[:, 0:16]),
             reads=[K("ps", 4)], writes=[("ltm", tt)])
    for tt in range(16):
        l = ltm.ap()[:, tt, :]
        m1 = r1.ap()[:, 0:1]; nm1 = r1.ap()[:, 1:2]; Z = r1.ap()[:, 2:3]
        rZ = r1.ap()[:, 3:4]; m2 = r1.ap()[:, 4:5]; e2 = r1.ap()[:, 5:6]
        P2 = r1.ap()[:, 6:7]; s_ = r1.ap()[:, 7:8]; rw_ = r1.ap()[:, 8:9]
        w1c = r1.ap()[:, 9:10]; w2c = r1.ap()[:, 10:11]
        pun = ctm.ap()[:, tt, :]
        RK = K("r1"); LK = ("ltm", tt); CK = ("ctm", tt)
        p.op("vector", lambda e, l=l, m1=m1: e.reduce_max(m1, l, axis=AX.X),
             reads=[LK, RK], writes=[RK])
        p.op("vector", lambda e, m1=m1, nm1=nm1: e.tensor_scalar(nm1, m1, -1.0, None, OP.mult),
             reads=[RK], writes=[RK])
        p.op("scalar", lambda e, l=l, pun=pun, nm1=nm1, Z=Z: e.activation(
            pun, l, AF.Exp, bias=nm1, scale=1.0, accum_out=Z),
            reads=[LK, RK], writes=[CK, RK])
        p.op("vector", lambda e, Z=Z, rZ=rZ: e.reciprocal(rZ, Z), reads=[RK], writes=[RK])
        msk = ltm.ap()[:, tt, :]
        p.op("vector", lambda e, l=l, pun=pun, m1=m1: e.tensor_scalar(pun, l, m1, None, OP.is_equal),
             reads=[LK, RK, CK], writes=[CK])
        p.op("vector", lambda e, l=l, pun=pun: e.scalar_tensor_tensor(
            l, pun, -3.0e38, l, OP.mult, OP.add),
            reads=[LK, CK], writes=[LK])
        p.op("vector", lambda e, l=l, m2=m2: e.reduce_max(m2, l, axis=AX.X),
             reads=[LK, RK], writes=[RK])
        p.op("scalar", lambda e, m2=m2, e2=e2, nm1=nm1: e.activation(
            e2, m2, AF.Exp, bias=nm1, scale=1.0),
            reads=[RK], writes=[RK])
        p.op("vector", lambda e, e2=e2, rZ=rZ, P2=P2: e.tensor_tensor(P2, e2, rZ, OP.mult),
             reads=[RK], writes=[RK])
        p.op("vector", lambda e, P2=P2, rZ=rZ, s_=s_: e.scalar_tensor_tensor(
            s_, P2, 1e-6, rZ, OP.add, OP.add),
            reads=[RK], writes=[RK])
        p.op("vector", lambda e, s_=s_, rw_=rw_: e.reciprocal(rw_, s_), reads=[RK], writes=[RK])
        p.op("vector", lambda e, rZ=rZ, rw_=rw_, w1c=w1c: e.tensor_tensor(w1c, rZ, rw_, OP.mult),
             reads=[RK], writes=[RK])
        p.op("vector", lambda e, P2=P2, rw_=rw_, w2c=w2c: e.tensor_tensor(w2c, P2, rw_, OP.mult),
             reads=[RK], writes=[RK])
        p.op("vector", lambda e, pun=pun, w1c=w1c: e.tensor_scalar(pun, pun, w1c, None, OP.mult),
             reads=[CK, RK], writes=[CK])
        p.op("vector", lambda e, l=l, m2=m2: e.tensor_scalar(l, l, m2, None, OP.is_equal),
             reads=[LK, RK], writes=[LK])
        p.op("vector", lambda e, l=l, pun=pun, w2c=w2c: e.scalar_tensor_tensor(
            pun, l, w2c, pun, OP.mult, OP.add),
            reads=[LK, RK, CK], writes=[CK])
    for tt in range(16):
        pb = ctx.psum[4]
        p.op("tensor", lambda e, tt=tt, pb=pb: e.transpose(
            pb[0:16, 0:128], ctm.ap()[:, tt, :], ctx.idnf[:]),
            reads=[("ctm", tt), K("idnf")], writes=[K("ps", 4)])
        p.op("vector", lambda e, tt=tt, pb=pb: e.tensor_copy(
            cfm.ap()[0:16, tt * 128:(tt + 1) * 128], pb[0:16, 0:128]),
            reads=[K("ps", 4)], writes=[("cfm", tt // 4)])
    for tc in range(4):
        pb = ctx.psum[5]
        p.op("tensor", lambda e, tc=tc, pb=pb: e.matmul(
            pb[:, 0:512], selrep.ap()[:, 15 * 128:16 * 128],
            cfm.ap()[0:16, tc * 512:(tc + 1) * 512],
            start=True, stop=True),
            reads=[("cfm", tc), K("selrep16")], writes=[K("ps", 5)])
        for g in range(G):
            xs = ctx.x.ap()[:, g, tc * 512:(tc + 1) * 512]
            p.op("vector", lambda e, xs=xs, pb=pb: e.scalar_tensor_tensor(
                xs, pb[:, 0:512], 1.0, xs, OP.add, OP.mult),
                reads=[K("ps", 5), ("x", g)], writes=[("x", g)])
    NEXP = int(_os.environ.get("KEXPN", "15"))
    NOCONV = bool(_os.environ.get("KNOCONV"))
    for ex in range(NEXP):
        bi = ex % 2
        WK = ("ew", bi)
        p.op("gpsimd", lambda e, ex=ex, bi=bi: e.dma_start(gw[bi].ap()[:, :, :], gat[ex]),
             reads=[WK], writes=[WK], dma=True)
        p.op("gpsimd", lambda e, ex=ex, bi=bi: e.dma_start(uw[bi].ap()[:, :, :], upd[ex]),
             reads=[WK], writes=[WK], dma=True)
        p.op("gpsimd", lambda e, ex=ex, bi=bi: e.dma_start(
            dw[bi].ap()[:, :, :], dnd[ex]),
            reads=[WK], writes=[WK], dma=True)
        for tc in range(4):
            pb = ctx.psum[5]
            p.op("tensor", lambda e, ex=ex, tc=tc, pb=pb: e.matmul(
                pb[:, 0:512], selrep.ap()[:, ex * 128:(ex + 1) * 128],
                cfm.ap()[0:16, tc * 512:(tc + 1) * 512],
                start=True, stop=True),
                reads=[("cfm", tc), K("selrep16")], writes=[K("ps", 5)])
            p.op("vector", lambda e, tc=tc, pb=pb: e.tensor_copy(
                crep.ap()[:, tc * 512:(tc + 1) * 512], pb[:, 0:512]),
                reads=[K("ps", 5)], writes=[("crep", tc)])
        for ft in range(4):
            for tc in range(4):
                pg = ctx.psum[3]; pu = ctx.psum[4]
                for g in range(G):
                    p.op("tensor", lambda e, g=g, ft=ft, tc=tc, bi=bi, pg=pg: e.matmul(
                        pg[:, 0:512], gw[bi].ap()[:, g, ft * 128:(ft + 1) * 128],
                        ctx.xb.ap()[:, g, PADL + tc * 512:PADL + (tc + 1) * 512],
                        start=(g == 0), stop=(g == G - 1)),
                        reads=[WK, ("xb", g)], writes=[K("ps", 3)])
                for g in range(G):
                    p.op("tensor", lambda e, g=g, ft=ft, tc=tc, bi=bi, pu=pu: e.matmul(
                        pu[:, 0:512], uw[bi].ap()[:, g, ft * 128:(ft + 1) * 128],
                        ctx.xb.ap()[:, g, PADL + tc * 512:PADL + (tc + 1) * 512],
                        start=(g == 0), stop=(g == G - 1)),
                        reads=[WK, ("xb", g)], writes=[K("ps", 4)])
                p.op("scalar", lambda e, pg=pg: e.activation(
                    sil.ap()[:], pg[:, 0:512], AF.Silu, bias=0.0, scale=1.0),
                    reads=[K("ps", 3), K("sil")], writes=[K("sil")])
                p.op("vector", lambda e, tc=tc, pu=pu: e.tensor_tensor(
                    tu.ap()[:], pu[:, 0:512], crep.ap()[:, tc * 512:(tc + 1) * 512], OP.mult),
                    reads=[K("ps", 4), ("crep", tc), K("tu")], writes=[K("tu")])
                p.op("vector", lambda e, ft=ft, tc=tc: e.tensor_tensor(
                    act.ap()[:, ft, tc * 512:(tc + 1) * 512], sil.ap()[:], tu.ap()[:], OP.mult),
                    reads=[K("sil"), K("tu")], writes=[("act", ft)])
        for dout in range(G):
            for tc in range(4):
                pb = ctx.psum[5 if dout % 2 == 0 else 6]
                pk = K("ps", 5 if dout % 2 == 0 else 6)
                for ft in range(4):
                    p.op("tensor", lambda e, ft=ft, dout=dout, tc=tc, bi=bi, pb=pb: e.matmul(
                        pb[:, 0:512], dw[bi].ap()[:, ft, dout * 128:(dout + 1) * 128],
                        act.ap()[:, ft, tc * 512:(tc + 1) * 512],
                        start=(ft == 0), stop=(ft == 3)),
                        reads=[WK, ("act", ft)], writes=[pk])
                xs = ctx.x.ap()[:, dout, tc * 512:(tc + 1) * 512]
                p.op("vector", lambda e, xs=xs, pb=pb: e.tensor_tensor(xs, xs, pb[:, 0:512], OP.add),
                     reads=[pk, ("x", dout)], writes=[("x", dout)])
    if NOCONV:
        return
    cw = gw[0]; cw2 = uw[0]
    cv = [gw[0], gw[1], uw[0]]
    for k in range(3):
        p.op("gpsimd", lambda e, k=k, b=cv[k]: e.dma_start(
            b.ap()[:, :, :], dr["l0f_conv"].rearrange("(g p) k f -> p g k f", p=128)[:, :, k, :]),
            reads=[], writes=[("cv", k), ("ew", 0), ("ew", 1)], dma=True)
    p.op("gpsimd", lambda e: e.dma_start(
        uw[1].ap()[:, :, :], dr["l0f_sup"].rearrange("(g p) f -> p g f", p=128)),
        reads=[], writes=[K("sup"), ("ew", 0), ("ew", 1)], dma=True)
    p.op("gpsimd", lambda e: e.dma_start(
        dw[0].ap()[:, :, :], dr["l0f_sdn"].rearrange("(c p) f -> p c f", p=128)),
        reads=[], writes=[K("sdn"), ("ew", 0), ("ew", 1)], dma=True)
    for ft in range(4):
        for tc in range(4):
            pg = ctx.psum[3]; pu = ctx.psum[4]
            for k in range(3):
                for g in range(G):
                    sh = k - 2
                    p.op("tensor", lambda e, g=g, k=k, ft=ft, tc=tc, sh=sh, pg=pg: e.matmul(
                        pg[:, 0:512], cv[k].ap()[:, g, ft * 128:(ft + 1) * 128],
                        ctx.xb.ap()[:, g, PADL + sh + tc * 512:PADL + sh + (tc + 1) * 512],
                        start=(k == 0 and g == 0), stop=(k == 2 and g == G - 1)),
                        reads=[("cv", k), ("xb", g)], writes=[K("ps", 3)])
            for g in range(G):
                p.op("tensor", lambda e, g=g, ft=ft, tc=tc, pu=pu: e.matmul(
                    pu[:, 0:512], uw[1].ap()[:, g, ft * 128:(ft + 1) * 128],
                    ctx.xb.ap()[:, g, PADL + tc * 512:PADL + (tc + 1) * 512],
                    start=(g == 0), stop=(g == G - 1)),
                    reads=[K("sup"), ("xb", g)], writes=[K("ps", 4)])
            p.op("scalar", lambda e, pg=pg: e.activation(
                sil.ap()[:], pg[:, 0:512], AF.Silu, bias=0.0, scale=1.0),
                reads=[K("ps", 3), K("sil")], writes=[K("sil")])
            p.op("vector", lambda e, ft=ft, tc=tc, pu=pu: e.tensor_tensor(
                act.ap()[:, ft, tc * 512:(tc + 1) * 512], sil.ap()[:], pu[:, 0:512], OP.mult),
                reads=[K("ps", 4), K("sil")], writes=[("act", ft)])
    for dout in range(G):
        for tc in range(4):
            pb = ctx.psum[5 if dout % 2 == 0 else 6]
            pk = K("ps", 5 if dout % 2 == 0 else 6)
            for ft in range(4):
                p.op("tensor", lambda e, ft=ft, dout=dout, tc=tc, pb=pb: e.matmul(
                    pb[:, 0:512], dw[0].ap()[:, ft, dout * 128:(dout + 1) * 128],
                    act.ap()[:, ft, tc * 512:(tc + 1) * 512],
                    start=(ft == 0), stop=(ft == 3)),
                    reads=[K("sdn"), ("act", ft)], writes=[pk])
            xs = ctx.x.ap()[:, dout, tc * 512:(tc + 1) * 512]
            p.op("vector", lambda e, xs=xs, pb=pb: e.tensor_tensor(xs, xs, pb[:, 0:512], OP.add),
                 reads=[pk, ("x", dout)], writes=[("x", dout)])

def emit_ln_weighted(p, ctx, es, lnp, which):
    nc = ctx.nc
    sq0 = es.enter_context(nc.sbuf_tensor(uname("sq0"), [128, 512], BF16, side="right"))
    sq1 = es.enter_context(nc.sbuf_tensor(uname("sq1"), [128, 512], BF16, side="right"))
    rA = es.enter_context(nc.sbuf_tensor(uname("rA"), [1, T], F32, side="right"))
    rB = es.enter_context(nc.sbuf_tensor(uname("rB"), [1, T], F32, side="right"))
    rC = es.enter_context(nc.sbuf_tensor(uname("rC"), [1, T], F32, side="right"))
    tf = es.enter_context(nc.sbuf_tensor(uname("tf"), [128, 512], F32, side="right"))
    epsc = es.enter_context(nc.sbuf_tensor(uname("epsc"), [1, 1], F32, side="right"))
    p.op("vector", lambda e: e.memset(epsc.ap()[:], 1e-5), reads=[], writes=[K("epsc")])
    for g in range(G):
        xg = ctx.x.ap()[:, g, :]
        p.op("scalar", lambda e, g=g, xg=xg: e.copy(
            ctx.xb.ap()[:, g, PADL:PADL + T], xg),
            reads=[("x", g)], writes=[("xb", g)])
    for tc in range(4):
        for g in range(G):
            p.op("tensor", lambda e, tc=tc, g=g: e.matmul(
                ctx.psum[4][0:1, 0:512], ctx.onescol[:],
                ctx.xb.ap()[:, g, PADL + tc * 512:PADL + (tc + 1) * 512],
                start=(g == 0), stop=(g == G - 1)),
                reads=[("xb", g), K("onescol")], writes=[K("ps", 4)])
        p.op("vector", lambda e, tc=tc: e.tensor_scalar(
            rA.ap()[:, tc * 512:(tc + 1) * 512], ctx.psum[4][0:1, 0:512],
            1.0 / D, None, OP.mult),
            reads=[K("ps", 4)], writes=[("rA", tc)])
        for g in range(G):
            sq = sq0 if g % 2 == 0 else sq1
            sk = ("sq", g % 2)
            p.op("vector", lambda e, g=g, tc=tc, sq=sq: e.tensor_tensor(
                sq[:], ctx.xb.ap()[:, g, PADL + tc * 512:PADL + (tc + 1) * 512],
                ctx.xb.ap()[:, g, PADL + tc * 512:PADL + (tc + 1) * 512], OP.mult),
                reads=[("xb", g), sk], writes=[sk])
            p.op("tensor", lambda e, g=g, sq=sq: e.matmul(
                ctx.psum[5][0:1, 0:512], ctx.onescol[:], sq[:],
                start=(g == 0), stop=(g == G - 1)),
                reads=[sk, K("onescol")], writes=[K("ps", 5)])
        p.op("vector", lambda e, tc=tc: e.tensor_scalar(
            rB.ap()[:, tc * 512:(tc + 1) * 512], ctx.psum[5][0:1, 0:512],
            1.0 / D, None, OP.mult),
            reads=[K("ps", 5)], writes=[("rB", tc)])
    RA = KS("rA", range(4)); RB = KS("rB", range(4))
    p.op("vector", lambda e: e.tensor_tensor(rC.ap()[:], rA.ap()[:], rA.ap()[:], OP.mult),
         reads=RA, writes=[K("rC")])
    p.op("vector", lambda e: e.tensor_tensor(rB.ap()[:], rB.ap()[:], rC.ap()[:], OP.subtract),
         reads=RB + [K("rC")], writes=[K("var")])
    p.op("scalar", lambda e: e.activation(rB.ap()[:], rB.ap()[:], AF.Sqrt, bias=epsc.ap()[:, 0:1], scale=1.0),
         reads=[K("var"), K("epsc")], writes=[K("sqv")])
    p.op("vector", lambda e: e.reciprocal(rB.ap()[:], rB.ap()[:]),
         reads=[K("sqv")], writes=[K("rstd")])
    p.op("vector", lambda e: e.tensor_tensor(rC.ap()[:], rA.ap()[:], rB.ap()[:], OP.mult),
         reads=RA + [K("rstd")], writes=[K("mr")])
    for tc in range(4):
        p.op("tensor", lambda e, tc=tc: e.matmul(
            ctx.psum[4][:, 0:512], ctx.ones1f[:], rB.ap()[:, tc * 512:(tc + 1) * 512],
            start=True, stop=True),
            reads=[K("rstd"), K("ones1f")], writes=[K("ps", 4)])
        p.op("tensor", lambda e, tc=tc: e.matmul(
            ctx.psum[5][:, 0:512], ctx.ones1f[:], rC.ap()[:, tc * 512:(tc + 1) * 512],
            start=True, stop=True),
            reads=[K("mr"), K("ones1f")], writes=[K("ps", 5)])
        for g in range(G):
            xs = ctx.x.ap()[:, g, tc * 512:(tc + 1) * 512]
            p.op("vector", lambda e, xs=xs, g=g: e.tensor_tensor(
                tf.ap()[:], xs, ctx.psum[4][:, 0:512], OP.mult),
                reads=[("x", g), K("ps", 4), K("tf")], writes=[K("tf")])
            p.op("vector", lambda e: e.tensor_tensor(
                tf.ap()[:], tf.ap()[:], ctx.psum[5][:, 0:512], OP.subtract),
                reads=[K("ps", 5), K("tf")], writes=[K("tf")])
            gcol = lnp.ap()[:, g * 4 + 2 * which:g * 4 + 2 * which + 1]
            bcol = lnp.ap()[:, g * 4 + 2 * which + 1:g * 4 + 2 * which + 2]
            p.op("scalar", lambda e, g=g, tc=tc, gcol=gcol, bcol=bcol: e.activation(
                ctx.xb.ap()[:, g, PADL + tc * 512:PADL + (tc + 1) * 512], tf.ap()[:],
                AF.Identity, bias=bcol, scale=gcol),
                reads=[K("tf"), K("lnp")], writes=[("xb", g)])

def bcast_ap(ap, dim, count):
    import copy as _copy
    ap2 = _copy.deepcopy(ap)
    ap2.ap.insert(dim, [0, count])
    return ap2

def emit_l3_ffn(p, ctx, es, dr, core_consts):
    nc = ctx.nc
    lnp = es.enter_context(nc.sbuf_tensor(uname("lnp"), [128, G * 4], F32, side="right"))
    p.op("sync", lambda e: e.dma_start(lnp[:], dr["l3f_ln"][:]), reads=[], writes=[K("lnp")], dma=True)
    xall0 = ctx.x.ap()[:, :, :].rearrange("p a b -> p (a b)")
    p.op("sync", lambda e, xall0=xall0: e.dma_start(ctx.xspill[:], xall0),
         reads=XK, writes=[K("xorig")], dma=True)
    with phase(p, es) as ph:
        emit_ln_weighted(p, ctx, ph, lnp, 0)
    dbg_dump(p, ctx, "ln1", ctx.xb.ap()[:, :, PADL:PADL + T], [128, G, T], BF16, XBK)
    es2 = ExitStack()
    try:
        qkv = es2.enter_context(nc.sbuf_tensor(uname("qkv"), [128, 16, 1536], BF16, side="right"))
        wqv = [es2.enter_context(nc.sbuf_tensor(uname(f"wqv{i}"), [128, G, 512], BF16, side="right")) for i in range(2)]
        kvg = [es2.enter_context(nc.sbuf_tensor(uname(f"kvg{i}"), [128, 8, 512], BF16, side="right")) for i in range(2)]
        sco = es2.enter_context(nc.sbuf_tensor(uname("sco"), [128, 16, 8], F32, side="right"))
        prodb = es2.enter_context(nc.sbuf_tensor(uname("prodb"), [128, 1024], BF16, side="right"))
        ssum = es2.enter_context(nc.sbuf_tensor(uname("ssum"), [128, 16], F32, side="right"))
        obuf = es2.enter_context(nc.sbuf_tensor(uname("obuf"), [128, 16, 64], BF16, side="right"))
        tb = es2.enter_context(nc.sbuf_tensor(uname("tb"), [128, 1024], BF16, side="right"))
        aot = es2.enter_context(nc.sbuf_tensor(uname("aot"), [128, 8, 128], BF16, side="right"))
        mrep = es2.enter_context(nc.sbuf_tensor(uname("mrep"), [128, 8], F32, side="right"))
        mrow = es2.enter_context(nc.sbuf_tensor(uname("mrow"), [1, 8], F32, side="right"))
        wpj = [es2.enter_context(nc.sbuf_tensor(uname(f"wpj{i}"), [128, G, 128], BF16, side="right")) for i in range(2)]
        p.op("sync", lambda e: e.dma_start(mrow[:], dr["l3f_mask"][:]), reads=[], writes=[K("mrow")], dma=True)
        p.op("tensor", lambda e: e.matmul(ctx.psum[6][:, 0:8], ctx.ones1f[:], mrow.ap()[:],
                                          start=True, stop=True),
             reads=[K("mrow"), K("ones1f")], writes=[K("ps", 6)])
        p.op("vector", lambda e: e.tensor_copy(mrep.ap()[:], ctx.psum[6][:, 0:8]),
             reads=[K("ps", 6)], writes=[K("mrep")])
        qkvw = dr["l3f_qkvw"].rearrange("(g p) n -> p g n", p=128)
        for nch in range(3):
            wb = wqv[nch % 2]
            wk = ("wqv", nch % 2)
            p.op("gpsimd", lambda e, nch=nch, wb=wb: e.dma_start(
                wb.ap()[:, :, :], qkvw[:, :, nch * 512:(nch + 1) * 512]),
                reads=[], writes=[wk], dma=True)
            for tt in range(16):
                pb = ctx.psum[3 + (tt % 2)]
                pk = K("ps", 3 + (tt % 2))
                for g in range(G):
                    p.op("tensor", lambda e, g=g, tt=tt, wb=wb, pb=pb: e.matmul(
                        pb[:, 0:512], ctx.xb.ap()[:, g, PADL + tt * 128:PADL + (tt + 1) * 128],
                        wb.ap()[:, g, :], start=(g == 0), stop=(g == G - 1)),
                        reads=[wk, ("xb", g)], writes=[pk])
                p.op("vector", lambda e, tt=tt, nch=nch, pb=pb: e.tensor_copy(
                    qkv.ap()[:, tt, nch * 512:(nch + 1) * 512], pb[:, 0:512]),
                    reads=[pk], writes=[("qkv", tt)])
        dbg_dump(p, ctx, "qkv", qkv.ap()[:, :, :], [128, 16, 1536], BF16,
                 [("qkv", t2) for t2 in range(16)])
        ccv = ctx.cc_kv_in.rearrange("(tt p) (kv g4 c) -> p tt kv g4 c", p=128, kv=2, g4=4)
        for g4 in range(4):
            p.op("sync", lambda e, g4=g4: e.dma_start(
                ccv[:, :, 0, g4, :], qkv.ap()[:, :, g4 * 384 + 256:g4 * 384 + 320]),
                reads=[("qkv", tt2) for tt2 in range(16)], writes=[K("ccin")], dma=True)
            p.op("sync", lambda e, g4=g4: e.dma_start(
                ccv[:, :, 1, g4, :], qkv.ap()[:, :, g4 * 384 + 320:g4 * 384 + 384]),
                reads=[("qkv", tt2) for tt2 in range(16)], writes=[K("ccin")], dma=True)
        p.op("gpsimd", lambda e: e.collective_compute(
            "AllGather", OP.bypass, replica_groups=[list(range(NC))],
            ins=[ctx.cc_kv_in_t.ap().opt()], outs=[ctx.cc_kv_out_t.ap().opt()]),
            reads=[K("ccin")], writes=[K("ccout")])
        for tt in range(16):
            kb = kvg[tt % 2]
            kk = ("kvg", tt % 2)
            p.op("sync", lambda e, tt=tt, kb=kb: e.dma_start(
                kb.ap()[:, :, :],
                ctx.cc_kv_out.rearrange("(s tt p) c -> p tt s c", p=128, tt=16)[:, tt, :, :]),
                reads=[kk, K("ccout")], writes=[kk], dma=True)
            if tt == 0:
                dbg_dump(p, ctx, "kvg0", kb.ap()[:, :, :], [128, 8, 512], BF16, [kk])
            qv = qkv.ap()[:, tt, :].rearrange("p (g4 c) -> p g4 c", g4=4)[:, :, 0:256]
            qv = qv.rearrange("p g4 (qh d) -> p g4 qh d", d=64)
            for sp in range(8):
                kvw = kb.ap()[:, sp, 0:256].rearrange("p (g4 d) -> p g4 d", g4=4)
                kvb = bcast_ap(kvw, 2, 4)
                pr4 = prodb.ap()[:].rearrange("p (g4 qh d) -> p g4 qh d", g4=4, qh=4)
                p.op("vector", lambda e, qv=qv, kvb=kvb, pr4=pr4: e.tensor_tensor(
                    pr4, qv, kvb, OP.mult),
                    reads=[("qkv", tt), kk, K("prodb")], writes=[K("prodb")])
                p.op("vector", lambda e, sp=sp: e.tensor_reduce(
                    sco.ap()[:, :, sp:sp + 1],
                    prodb.ap()[:].rearrange("p (gq d) -> p gq d", d=64),
                    axis=AX.X, op=OP.add),
                    reads=[K("prodb"), K("sco")], writes=[K("sco")])
            mb = bcast_ap(mrep.ap()[:], 1, 16)
            p.op("vector", lambda e, mb=mb: e.tensor_tensor(
                sco.ap()[:, :, :], sco.ap()[:, :, :], mb, OP.add),
                reads=[K("sco"), K("mrep")], writes=[K("sco")])
            p.op("scalar", lambda e: e.activation(
                sco.ap()[:, :, :], sco.ap()[:, :, :], AF.Exp, bias=0.0, scale=0.125),
                reads=[K("sco")], writes=[K("sco")])
            p.op("vector", lambda e: e.tensor_reduce(
                ssum.ap()[:], sco.ap()[:, :, :], axis=AX.X, op=OP.add),
                reads=[K("sco"), K("ssum")], writes=[K("ssum")])
            p.op("vector", lambda e: e.reciprocal(ssum.ap()[:], ssum.ap()[:]),
                 reads=[K("ssum")], writes=[K("ssum")])
            sb_ = bcast_ap(ssum.ap()[:], 2, 8)
            p.op("vector", lambda e, sb_=sb_: e.tensor_tensor(
                sco.ap()[:, :, :], sco.ap()[:, :, :], sb_, OP.mult),
                reads=[K("sco"), K("ssum")], writes=[K("sco")])
            for sp in range(8):
                import copy as _copy
                pv = _copy.deepcopy(sco.ap()[:, :, sp:sp + 1])
                pv.ap[-1] = [0, 64]
                vvw = kb.ap()[:, sp, 256:512].rearrange("p (g4 d) -> p g4 d", g4=4)
                vvb = bcast_ap(vvw, 2, 4)
                ob4 = obuf.ap()[:, :, :].rearrange("p (g4 qh) d -> p g4 qh d", qh=4)
                pv4 = _copy.deepcopy(pv); pv4.ap.insert(1, pv4.ap.pop(1))
                tb4 = tb.ap()[:].rearrange("p (g4 qh d) -> p g4 qh d", g4=4, qh=4)
                pvb = _copy.deepcopy(sco.ap()[:, :, sp:sp + 1].rearrange("p (g4 qh) b -> p g4 qh b", qh=4))
                pvb.ap[-1] = [0, 64]
                if sp == 0:
                    p.op("vector", lambda e, pvb=pvb, vvb=vvb, ob4=ob4: e.tensor_tensor(
                        ob4, pvb, vvb, OP.mult),
                        reads=[K("sco"), kk, K("obuf")], writes=[K("obuf")])
                else:
                    p.op("vector", lambda e, pvb=pvb, vvb=vvb, tb4=tb4: e.tensor_tensor(
                        tb4, pvb, vvb, OP.mult),
                        reads=[K("sco"), kk, K("tbb")], writes=[K("tbb")])
                    p.op("vector", lambda e, ob4=ob4, tb4=tb4: e.tensor_tensor(
                        ob4, ob4, tb4, OP.add),
                        reads=[K("tbb"), K("obuf")], writes=[K("obuf")])
            if tt == 0:
                dbg_dump(p, ctx, "sco0", sco.ap()[:, :, :], [128, 16, 8], F32, [K("sco")])
                dbg_dump(p, ctx, "obuf0", obuf.ap()[:, :, :], [128, 16, 64], BF16, [K("obuf")])
            for g4t in range(8):
                p.op("tensor", lambda e, g4t=g4t: e.transpose(
                    ctx.psT[:, 0:128],
                    obuf.ap()[:, :, :].rearrange("p a b -> p (a b)")[:, g4t * 128:(g4t + 1) * 128],
                    ctx.idn[:]),
                    reads=[K("obuf"), K("idn")], writes=[K("psT")])
                p.op("vector", lambda e, g4t=g4t: e.tensor_copy(
                    aot.ap()[:, g4t, :], ctx.psT[:, 0:128]),
                    reads=[K("psT")], writes=[K("aot")])
            for ge in range(G):
                wb = wpj[ge % 2]
                wk2 = ("wpj", ge % 2)
                if tt == 0 or True:
                    pass
                p.op("gpsimd", lambda e, ge=ge, wb=wb: e.dma_start(
                    wb.ap()[:, :, :], dr["l3f_projw"].rearrange("(g p) n -> p g n", p=128)[:, :, ge * 128:(ge + 1) * 128]),
                    reads=[wk2], writes=[wk2], dma=True)
                pb = ctx.psum[5]
                for g in range(G):
                    p.op("tensor", lambda e, g=g, ge=ge, wb=wb, pb=pb: e.matmul(
                        pb[:, 0:128], wb.ap()[:, g, :], aot.ap()[:, g, :],
                        start=(g == 0), stop=(g == G - 1)),
                        reads=[wk2, K("aot")], writes=[K("ps", 5)])
                xs = ctx.x.ap()[:, ge, tt * 128:(tt + 1) * 128]
                p.op("vector", lambda e, xs=xs, ge=ge, pb=pb: e.tensor_tensor(
                    xs, xs, pb[:, 0:128], OP.add),
                    reads=[K("ps", 5), ("x", ge)], writes=[("x", ge)])
    finally:
        es2.close()
    p.barrier()
    dbg_dump(p, ctx, "h1", ctx.x.ap()[:, :, :], [128, G, T], F32, XK)
    with phase(p, es) as ph:
        emit_ln_weighted(p, ctx, ph, lnp, 1)
    rtr = ExitStack()
    try:
        selrep = rtr.enter_context(nc.sbuf_tensor(uname("selrep"), [8, 8 * 128], F32, side="right"))
        pfm = rtr.enter_context(nc.sbuf_tensor(uname("pfm"), [8, T], F32, side="right"))
        cfm = rtr.enter_context(nc.sbuf_tensor(uname("cfm"), [8, T], F32, side="right"))
        selc = rtr.enter_context(nc.sbuf_tensor(uname("selc"), [8, 1], F32, side="right"))
        tots = rtr.enter_context(nc.sbuf_tensor(uname("tots"), [8, 8], F32, side="right"))
        offs = rtr.enter_context(nc.sbuf_tensor(uname("offs"), [8, 1], F32, side="right"))
        rtr2 = ExitStack()
        rwb = rtr2.enter_context(nc.sbuf_tensor(uname("rwb"), [128, G, 8], BF16, side="right"))
        lfm = rtr2.enter_context(nc.sbuf_tensor(uname("lfm"), [8, T], F32, side="right"))
        ltm = rtr2.enter_context(nc.sbuf_tensor(uname("ltm"), [128, 16, 8], F32, side="right"))
        ctm = rtr2.enter_context(nc.sbuf_tensor(uname("ctm"), [128, 16, 8], F32, side="right"))
        ntm = rtr2.enter_context(nc.sbuf_tensor(uname("ntm"), [128, 16, 8], F32, side="right"))
        wfm = rtr2.enter_context(nc.sbuf_tensor(uname("wfm"), [8, T], F32, side="right"))
        r1 = rtr2.enter_context(nc.sbuf_tensor(uname("r1"), [128, 12], F32, side="right"))
        p.op("gpsimd", lambda e: e.dma_start(rwb.ap()[:, :, :], dr["l3f_rw"].rearrange("(g p) n -> p g n", p=128)),
             reads=[], writes=[K("rwb")], dma=True)
        p.op("sync", lambda e: e.dma_start(selrep[:], dr["selrep8"][:]), reads=[], writes=[K("selrep8")], dma=True)
        p.op("sync", lambda e: e.dma_start(selc[:], dr["l3f_selcol"][:]), reads=[], writes=[K("selc")], dma=True)
        for tc in range(4):
            pb = ctx.psum[3]
            for g in range(G):
                p.op("tensor", lambda e, g=g, tc=tc, pb=pb: e.matmul(
                    pb[0:8, 0:512], rwb.ap()[:, g, :],
                    ctx.xb.ap()[:, g, PADL + tc * 512:PADL + (tc + 1) * 512],
                    start=(g == 0), stop=(g == G - 1)),
                    reads=[K("rwb"), ("xb", g)], writes=[K("ps", 3)])
            p.op("vector", lambda e, tc=tc, pb=pb: e.tensor_copy(
                lfm.ap()[0:8, tc * 512:(tc + 1) * 512], pb[0:8, 0:512]),
                reads=[K("ps", 3)], writes=[("lfm", tc)])
        for tt in range(16):
            pb = ctx.psum[4]
            p.op("tensor", lambda e, tt=tt, pb=pb: e.transpose(
                pb[:, 0:8], lfm.ap()[0:8, tt * 128:(tt + 1) * 128], ctx.idnf[0:8, 0:8]),
                reads=[("lfm", tt // 4), K("idnf")], writes=[K("ps", 4)])
            p.op("vector", lambda e, tt=tt, pb=pb: e.tensor_copy(ltm.ap()[:, tt, :], pb[:, 0:8]),
                 reads=[K("ps", 4)], writes=[("ltm", tt)])
        for tt in range(16):
            l = ltm.ap()[:, tt, :]
            m1 = r1.ap()[:, 0:1]; nm1 = r1.ap()[:, 1:2]; Z = r1.ap()[:, 2:3]
            rZ = r1.ap()[:, 3:4]; m2 = r1.ap()[:, 4:5]; e2 = r1.ap()[:, 5:6]
            P2 = r1.ap()[:, 6:7]; s_ = r1.ap()[:, 7:8]; rw_ = r1.ap()[:, 8:9]
            w1c = r1.ap()[:, 9:10]; w2c = r1.ap()[:, 10:11]
            pun = ctm.ap()[:, tt, :]
            cnt = ntm.ap()[:, tt, :]
            RK = K("r1"); LK = ("ltm", tt); CK = ("ctm", tt); NK = ("ntm", tt)
            p.op("vector", lambda e, l=l, m1=m1: e.reduce_max(m1, l, axis=AX.X),
                 reads=[LK, RK], writes=[RK])
            p.op("vector", lambda e, m1=m1, nm1=nm1: e.tensor_scalar(nm1, m1, -1.0, None, OP.mult),
                 reads=[RK], writes=[RK])
            p.op("scalar", lambda e, l=l, pun=pun, nm1=nm1, Z=Z: e.activation(
                pun, l, AF.Exp, bias=nm1, scale=1.0, accum_out=Z),
                reads=[LK, RK], writes=[CK, RK])
            p.op("vector", lambda e, Z=Z, rZ=rZ: e.reciprocal(rZ, Z), reads=[RK], writes=[RK])
            p.op("vector", lambda e, l=l, pun=pun, m1=m1: e.tensor_scalar(pun, l, m1, None, OP.is_equal),
                 reads=[LK, RK, CK], writes=[CK])
            p.op("vector", lambda e, l=l, pun=pun: e.scalar_tensor_tensor(
                l, pun, -3.0e38, l, OP.mult, OP.add),
                reads=[LK, CK], writes=[LK])
            p.op("vector", lambda e, l=l, m2=m2: e.reduce_max(m2, l, axis=AX.X),
                 reads=[LK, RK], writes=[RK])
            p.op("scalar", lambda e, m2=m2, e2=e2, nm1=nm1: e.activation(
                e2, m2, AF.Exp, bias=nm1, scale=1.0),
                reads=[RK], writes=[RK])
            p.op("vector", lambda e, e2=e2, rZ=rZ, P2=P2: e.tensor_tensor(P2, e2, rZ, OP.mult),
                 reads=[RK], writes=[RK])
            p.op("vector", lambda e, P2=P2, rZ=rZ, s_=s_: e.tensor_tensor(s_, P2, rZ, OP.add),
                 reads=[RK], writes=[RK])
            p.op("vector", lambda e, s_=s_, rw_=rw_: e.reciprocal(rw_, s_), reads=[RK], writes=[RK])
            p.op("vector", lambda e, rZ=rZ, rw_=rw_, w1c=w1c: e.tensor_tensor(w1c, rZ, rw_, OP.mult),
                 reads=[RK], writes=[RK])
            p.op("vector", lambda e, P2=P2, rw_=rw_, w2c=w2c: e.tensor_tensor(w2c, P2, rw_, OP.mult),
                 reads=[RK], writes=[RK])
            p.op("vector", lambda e, l=l, m2=m2: e.tensor_scalar(l, l, m2, None, OP.is_equal),
                 reads=[LK, RK], writes=[LK])
            p.op("vector", lambda e, l=l, pun=pun, cnt=cnt: e.tensor_tensor(cnt, pun, l, OP.add),
                 reads=[LK, CK, NK], writes=[NK])
            p.op("vector", lambda e, pun=pun, w1c=w1c: e.tensor_scalar(pun, pun, w1c, None, OP.mult),
                 reads=[CK, RK], writes=[CK])
            p.op("vector", lambda e, l=l, pun=pun, w2c=w2c: e.scalar_tensor_tensor(
                pun, l, w2c, pun, OP.mult, OP.add),
                reads=[LK, RK, CK], writes=[CK])
        for src_, dst_, dk in ((ctm, pfm, "pfm"), (ntm, cfm, "cfm")):
            for tt in range(16):
                pb = ctx.psum[4]
                p.op("tensor", lambda e, tt=tt, src_=src_, pb=pb: e.transpose(
                    pb[0:8, 0:128], src_.ap()[:, tt, :], ctx.idnf[:]),
                    reads=[("ctm", tt), ("ntm", tt), K("idnf")], writes=[K("ps", 4)])
                p.op("vector", lambda e, tt=tt, dst_=dst_, pb=pb: e.tensor_copy(
                    dst_.ap()[0:8, tt * 128:(tt + 1) * 128], pb[0:8, 0:128]),
                    reads=[K("ps", 4)], writes=[(dk,)])
        p.op("vector", lambda e: e.tensor_tensor_scan(
            wfm.ap()[0:8, :], cfm.ap()[0:8, :], cfm.ap()[0:8, :], 0.0, OP.add, OP.bypass),
            reads=[("cfm",)], writes=[K("wfm")])
        p.op("sync", lambda e: e.dma_start(ctx.cc_ct_in[:], wfm.ap()[0:8, T - 1:T]),
             reads=[K("wfm")], writes=[K("ctin")], dma=True)
        p.op("gpsimd", lambda e: e.collective_compute(
            "AllGather", OP.bypass, replica_groups=[list(range(NC))],
            ins=[ctx.cc_ct_in_t.ap().opt()], outs=[ctx.cc_ct_out_t.ap().opt()]),
            reads=[K("ctin")], writes=[K("ctout")])
        p.op("sync", lambda e: e.dma_start(
            tots[:], ctx.cc_ct_out.rearrange("(c e) one -> c (e one)", e=8)),
            reads=[K("ctout")], writes=[K("tots")], dma=True)
        p.op("tensor", lambda e: e.matmul(ctx.psum[6][0:8, 0:1], tots.ap()[:], selc.ap()[:],
                                          start=True, stop=True),
             reads=[K("tots"), K("selc")], writes=[K("ps", 6)])
        p.op("vector", lambda e: e.tensor_copy(offs.ap()[:], ctx.psum[6][0:8, 0:1]),
             reads=[K("ps", 6)], writes=[K("offs")])
        p.op("vector", lambda e: e.tensor_tensor(wfm.ap()[0:8, :], wfm.ap()[0:8, :], cfm.ap()[0:8, :], OP.subtract),
             reads=[K("wfm"), ("cfm",)], writes=[K("wfm")])
        p.op("vector", lambda e: e.tensor_scalar(
            cfm.ap()[0:8, :], wfm.ap()[0:8, :], offs.ap()[:], 5120.0, OP.add, OP.is_lt),
            reads=[K("wfm"), K("offs"), ("cfm",)], writes=[("cfm",)])
        p.op("vector", lambda e: e.tensor_tensor(pfm.ap()[0:8, :], pfm.ap()[0:8, :], cfm.ap()[0:8, :], OP.mult),
             reads=[("cfm",), ("pfm",)], writes=[("pfm",)])
        p.barrier()
        rtr2.close()
        act = rtr.enter_context(nc.sbuf_tensor(uname("act"), [128, 16, 1024], BF16, side="right"))
        w1b = [rtr.enter_context(nc.sbuf_tensor(uname(f"w1b{i}"), [128, G, 1024], BF16, side="right")) for i in range(2)]
        w2b = [rtr.enter_context(nc.sbuf_tensor(uname(f"w2b{i}"), [128, 16, 128], BF16, side="right")) for i in range(2)]
        crep = rtr.enter_context(nc.sbuf_tensor(uname("crep"), [128, 1024], F32, side="right"))
        gel = rtr.enter_context(nc.sbuf_tensor(uname("gel"), [128, 512], F32, side="right"))
        w1d = dr["l3f_w1"].rearrange("e (g p) f -> e p g f", p=128)
        w2d = dr["l3f_w2"].rearrange("e (c p) f -> e p c f", p=128)
        for th in range(2):
            for ex in range(8):
                for q in range(2):
                    pb = ctx.psum[5]
                    p.op("tensor", lambda e, ex=ex, th=th, q=q, pb=pb: e.matmul(
                        pb[:, 0:512], selrep.ap()[:, ex * 128:(ex + 1) * 128],
                        pfm.ap()[0:8, th * 1024 + q * 512:th * 1024 + (q + 1) * 512],
                        start=True, stop=True),
                        reads=[("pfm",), K("selrep8")], writes=[K("ps", 5)])
                    p.op("vector", lambda e, q=q, pb=pb: e.tensor_copy(
                        crep.ap()[:, q * 512:(q + 1) * 512], pb[:, 0:512]),
                        reads=[K("ps", 5)], writes=[("crep", q)])
                for half in range(2):
                    wb = w1b[half]
                    wk = ("w1b", half)
                    p.op("gpsimd", lambda e, ex=ex, half=half, wb=wb: e.dma_start(
                        wb.ap()[:, :, :], w1d[ex, :, :, half * 1024:(half + 1) * 1024]),
                        reads=[], writes=[wk], dma=True)
                    for fi in range(8):
                        fc = half * 8 + fi
                        for q in range(2):
                            pa = ctx.psum[3 + (q % 2)]
                            pk = K("ps", 3 + (q % 2))
                            for g in range(G):
                                p.op("tensor", lambda e, g=g, fi=fi, th=th, q=q, wb=wb, pa=pa: e.matmul(
                                    pa[:, 0:512], wb.ap()[:, g, fi * 128:(fi + 1) * 128],
                                    ctx.xb.ap()[:, g, PADL + th * 1024 + q * 512:PADL + th * 1024 + (q + 1) * 512],
                                    start=(g == 0), stop=(g == G - 1)),
                                    reads=[wk, ("xb", g)], writes=[pk])
                            p.op("scalar", lambda e, pa=pa: e.activation(
                                gel.ap()[:], pa[:, 0:512], AF.Gelu_apprx_tanh, bias=0.0, scale=1.0),
                                reads=[pk, K("gel")], writes=[K("gel")])
                            p.op("vector", lambda e, fc=fc, q=q: e.tensor_tensor(
                                act.ap()[:, fc, q * 512:(q + 1) * 512], gel.ap()[:],
                                crep.ap()[:, q * 512:(q + 1) * 512], OP.mult),
                                reads=[K("gel"), ("crep", q)], writes=[("act", fc)])
                for dout in range(G):
                    wk2 = ("w2b", dout % 2)
                    p.op("gpsimd", lambda e, ex=ex, dout=dout: e.dma_start(
                        w2b[dout % 2].ap()[:, :, :], w2d[ex, :, :, dout * 128:(dout + 1) * 128]),
                        reads=[], writes=[wk2], dma=True)
                    for q in range(2):
                        pb = ctx.psum[5 if q == 0 else 6]
                        pk = K("ps", 5 if q == 0 else 6)
                        for fc in range(16):
                            p.op("tensor", lambda e, fc=fc, dout=dout, q=q, pb=pb: e.matmul(
                                pb[:, 0:512], w2b[dout % 2].ap()[:, fc, :],
                                act.ap()[:, fc, q * 512:(q + 1) * 512],
                                start=(fc == 0), stop=(fc == 15)),
                                reads=[wk2, ("act", fc)], writes=[pk])
                        xs = ctx.x.ap()[:, dout, th * 1024 + q * 512:th * 1024 + (q + 1) * 512]
                        p.op("vector", lambda e, xs=xs, dout=dout, pb=pb: e.tensor_tensor(
                            xs, xs, pb[:, 0:512], OP.add),
                            reads=[pk, ("x", dout)], writes=[("x", dout)])
        xrb = [rtr.enter_context(nc.sbuf_tensor(uname(f"xrb{i}"), [128, 512], F32, side="right")) for i in range(2)]
        for g in range(G):
            for tc in range(4):
                i2 = (g * 4 + tc) % 2
                xk2 = ("xrb", i2)
                p.op("sync", lambda e, g=g, tc=tc, i2=i2: e.dma_start(
                    xrb[i2][:], ctx.xspill.rearrange("p (a b) -> p a b", a=G)[:, g, tc * 512:(tc + 1) * 512]),
                    reads=[xk2, K("xorig")], writes=[xk2], dma=True)
                xs = ctx.x.ap()[:, g, tc * 512:(tc + 1) * 512]
                p.op("vector", lambda e, xs=xs, i2=i2, g=g: e.tensor_tensor(
                    xs, xs, xrb[i2][:], OP.add),
                    reads=[("x", g), ("xrb", i2)], writes=[("x", g), ("xrb", i2)])
    finally:
        rtr.close()
    p.barrier()

def emit_l3_attn(p, ctx, es, dr):
    nc = ctx.nc
    H = 8
    scale = 1.0 / np.sqrt(160.0)
    CSHIFT = 10.0
    xall = ctx.x.ap()[:, :, :].rearrange("p a b -> p (a b)")
    p.op("sync", lambda e, xall=xall: e.dma_start(ctx.xspill[:], xall),
         reads=XK, writes=[K("xspdram")], dma=True)
    p.barrier()
    ctx.xstack.close()
    acc = es.enter_context(nc.sbuf_tensor(uname("acc"), [128, G, T], BF16, side="right"))
    beff = es.enter_context(nc.sbuf_tensor(uname("beff"), [128, G], F32, side="right"))
    es2 = ExitStack()
    kc = es2.enter_context(nc.sbuf_tensor(uname("kc"), [128, H, T], BF16, side="right"))
    vtm = es2.enter_context(nc.sbuf_tensor(uname("vtm"), [128, 16, H, 130], BF16, side="right"))
    kr = es2.enter_context(nc.sbuf_tensor(uname("kr"), [32, T], BF16, side="right"))
    cq = es2.enter_context(nc.sbuf_tensor(uname("cq"), [32, T], BF16, side="right"))
    ckv = es2.enter_context(nc.sbuf_tensor(uname("ckv"), [32, T], BF16, side="right"))
    wdq = es2.enter_context(nc.sbuf_tensor(uname("wdq"), [128, G, 64], BF16, side="right"))
    wu = es2.enter_context(nc.sbuf_tensor(uname("wu"), [32, 3 * 1024 + 256], BF16, side="right"))
    wkrb = es2.enter_context(nc.sbuf_tensor(uname("wkrb"), [128, G, 32], BF16, side="right"))
    wo = [es2.enter_context(nc.sbuf_tensor(uname(f"wo{i}"), [128, G, 128], BF16, side="right")) for i in range(2)]
    bd = es2.enter_context(nc.sbuf_tensor(uname("bd"), [32, 2], F32, side="right"))
    buqk = es2.enter_context(nc.sbuf_tensor(uname("buqk"), [128, 16], F32, side="right"))
    bqr = es2.enter_context(nc.sbuf_tensor(uname("bqr"), [32, 8], F32, side="right"))
    bkr = es2.enter_context(nc.sbuf_tensor(uname("bkr"), [32, 1], F32, side="right"))
    bvp = es2.enter_context(nc.sbuf_tensor(uname("bvp"), [128, G], BF16, side="right"))
    csh = es2.enter_context(nc.sbuf_tensor(uname("csh"), [128, 1], F32, side="right"))
    p.op("gpsimd", lambda e: e.dma_start(wdq.ap()[:, :, 0:32], dr["l3a_wdq"].rearrange("(g p) n -> p g n", p=128)),
         reads=[], writes=[K("wdq")], dma=True)
    p.op("gpsimd", lambda e: e.dma_start(wdq.ap()[:, :, 32:64], dr["l3a_wdkv"].rearrange("(g p) n -> p g n", p=128)),
         reads=[], writes=[K("wdq")], dma=True)
    p.op("gpsimd", lambda e: e.dma_start(wu.ap()[0:32, 0:1024], dr["l3a_wuq"][:]),
         reads=[], writes=[K("wu")], dma=True)
    p.op("gpsimd", lambda e: e.dma_start(wu.ap()[0:32, 1024:2048], dr["l3a_wuk"][:]),
         reads=[], writes=[K("wu")], dma=True)
    p.op("gpsimd", lambda e: e.dma_start(wu.ap()[0:32, 2048:3072], dr["l3a_wuv"][:]),
         reads=[], writes=[K("wu")], dma=True)
    p.op("gpsimd", lambda e: e.dma_start(wu.ap()[0:32, 3072:3328], dr["l3a_wqr"][:]),
         reads=[], writes=[K("wu")], dma=True)
    p.op("gpsimd", lambda e: e.dma_start(wkrb.ap()[:, :, :], dr["l3a_wkr"].rearrange("(g p) n -> p g n", p=128)),
         reads=[], writes=[K("wkrb")], dma=True)
    p.op("gpsimd", lambda e: e.dma_start(bvp.ap()[:, :], dr["l3a_bvp"][:]),
         reads=[], writes=[K("bvp")], dma=True)
    p.op("sync", lambda e: e.dma_start(bd[:], dr["l3a_bd"][:]), reads=[], writes=[K("bd")], dma=True)
    p.op("sync", lambda e: e.dma_start(buqk[:], dr["l3a_buqk"][:]), reads=[], writes=[K("buqk")], dma=True)
    p.op("sync", lambda e: e.dma_start(bqr[:], dr["l3a_bqr"][:]), reads=[], writes=[K("bqr")], dma=True)
    p.op("sync", lambda e: e.dma_start(bkr[:], dr["l3a_bkr"][:]), reads=[], writes=[K("bkr")], dma=True)
    p.op("sync", lambda e: e.dma_start(beff[:], dr["l3a_bo"][:]), reads=[], writes=[K("beff")], dma=True)
    p.op("vector", lambda e: e.memset(csh.ap()[:], -CSHIFT), reads=[], writes=[K("csh")])
    p.op("vector", lambda e: e.memset(vtm.ap()[:, :, :, 64:65], 1.0), reads=[], writes=[K("vones")])
    for part, col, dk in ((0, 0, "cq"), (1, 1, "ckv")):
        dst = cq if part == 0 else ckv
        for tc in range(4):
            pb = ctx.psum[3]
            for g in range(G):
                p.op("tensor", lambda e, g=g, tc=tc, part=part, pb=pb: e.matmul(
                    pb[0:32, 0:512], wdq.ap()[:, g, part * 32:(part + 1) * 32],
                    ctx.xb.ap()[:, g, PADL + tc * 512:PADL + (tc + 1) * 512],
                    start=(g == 0), stop=(g == G - 1)),
                    reads=[K("wdq"), ("xb", g)], writes=[K("ps", 3)])
            p.op("scalar", lambda e, tc=tc, dst=dst, col=col, pb=pb: e.activation(
                dst.ap()[0:32, tc * 512:(tc + 1) * 512], pb[0:32, 0:512],
                AF.Identity, bias=bd.ap()[0:32, col:col + 1], scale=1.0),
                reads=[K("ps", 3), K("bd")], writes=[(dk,)])
    for tc in range(4):
        pb = ctx.psum[3]
        for g in range(G):
            p.op("tensor", lambda e, g=g, tc=tc, pb=pb: e.matmul(
                pb[0:32, 0:512], wkrb.ap()[:, g, :],
                ctx.xb.ap()[:, g, PADL + tc * 512:PADL + (tc + 1) * 512],
                start=(g == 0), stop=(g == G - 1)),
                reads=[K("wkrb"), ("xb", g)], writes=[K("ps", 3)])
        p.op("scalar", lambda e, tc=tc, pb=pb: e.activation(
            kr.ap()[0:32, tc * 512:(tc + 1) * 512], pb[0:32, 0:512],
            AF.Identity, bias=bkr.ap()[0:32, 0:1], scale=1.0),
            reads=[K("ps", 3), K("bkr")], writes=[K("kr")])
    for h in range(H):
        for tc in range(4):
            pb = ctx.psum[4]
            p.op("tensor", lambda e, h=h, tc=tc, pb=pb: e.matmul(
                pb[:, 0:512], wu.ap()[0:32, 1024 + h * 128:1024 + (h + 1) * 128],
                ckv.ap()[0:32, tc * 512:(tc + 1) * 512], start=True, stop=True),
                reads=[K("wu"), ("ckv",)], writes=[K("ps", 4)])
            p.op("scalar", lambda e, h=h, tc=tc, pb=pb: e.activation(
                kc.ap()[:, h, tc * 512:(tc + 1) * 512], pb[:, 0:512],
                AF.Identity, bias=buqk.ap()[:, 8 + h:8 + h + 1], scale=1.0),
                reads=[K("ps", 4), K("buqk")], writes=[("kc", h)])
    for vt in range(16):
        for h in range(H):
            pb = ctx.psum[4]
            for half in range(2):
                p.op("tensor", lambda e, vt=vt, h=h, half=half, pb=pb: e.matmul(
                    pb[:, half * 64:(half + 1) * 64],
                    ckv.ap()[0:32, vt * 128:(vt + 1) * 128],
                    wu.ap()[0:32, 2048 + h * 128 + half * 64:2048 + h * 128 + (half + 1) * 64],
                    start=(half == 0), stop=(half == 1)),
                    reads=[K("wu"), ("ckv",)], writes=[K("ps", 4)])
            p.op("vector", lambda e, vt=vt, h=h, pb=pb: e.tensor_copy(
                vtm.ap()[:, vt, h, 0:64], pb[:, 0:64]),
                reads=[K("ps", 4)], writes=[("vtm", vt)])
            p.op("vector", lambda e, vt=vt, h=h, pb=pb: e.tensor_copy(
                vtm.ap()[:, vt, h, 65:129], pb[:, 64:128]),
                reads=[K("ps", 4)], writes=[("vtm", vt)])
    for ge in range(G):
        wob = wo[ge % 2]
        wkk = ("wo", ge % 2)
        p.op("gpsimd", lambda e, ge=ge, wob=wob: e.dma_start(
            wob.ap()[:, :, :], dr["l3a_wo"].rearrange("(g p) n -> p g n", p=128)[:, :, ge * 128:(ge + 1) * 128]),
            reads=[wkk], writes=[wkk], dma=True)
        pb = ctx.psum[6]
        for g in range(G):
            p.op("tensor", lambda e, g=g, ge=ge, wob=wob, pb=pb: e.matmul(
                pb[:, 0:1], wob.ap()[:, g, :], bvp.ap()[:, g:g + 1],
                start=(g == 0), stop=(g == G - 1)),
                reads=[wkk, K("bvp")], writes=[K("ps", 6)])
        p.op("vector", lambda e, ge=ge, pb=pb: e.tensor_tensor(
            beff.ap()[:, ge:ge + 1], beff.ap()[:, ge:ge + 1], pb[:, 0:1], OP.add),
            reads=[K("ps", 6), K("beff")], writes=[K("beff"), ("beffd", ge)])
    qcc = es2.enter_context(nc.sbuf_tensor(uname("qcc"), [128, H, 512], BF16, side="right"))
    qrc = es2.enter_context(nc.sbuf_tensor(uname("qrc"), [32, H, 512], BF16, side="right"))
    expS = es2.enter_context(nc.sbuf_tensor(uname("expS"), [128, 16, 512], BF16, side="right"))
    att = es2.enter_context(nc.sbuf_tensor(uname("att"), [128, H, 512], BF16, side="right"))
    arow = es2.enter_context(nc.sbuf_tensor(uname("arow"), [64, 512], F32, side="right"))
    rden = es2.enter_context(nc.sbuf_tensor(uname("rden"), [1, 512], F32, side="right"))
    anrm = es2.enter_context(nc.sbuf_tensor(uname("anrm"), [64, 512], BF16, side="right"))
    for qc in range(4):
        for h in range(H):
            pb = ctx.psum[3]
            p.op("tensor", lambda e, h=h, qc=qc, pb=pb: e.matmul(
                pb[:, 0:512], wu.ap()[0:32, h * 128:(h + 1) * 128],
                cq.ap()[0:32, qc * 512:(qc + 1) * 512], start=True, stop=True),
                reads=[K("wu"), ("cq",)], writes=[K("ps", 3)])
            p.op("scalar", lambda e, h=h, qc=qc, pb=pb: e.activation(
                qcc.ap()[:, h, :], pb[:, 0:512], AF.Identity,
                bias=buqk.ap()[:, h:h + 1], scale=1.0),
                reads=[K("ps", 3), K("buqk")], writes=[("qcc", h)])
            pb2 = ctx.psum[4]
            p.op("tensor", lambda e, h=h, qc=qc, pb2=pb2: e.matmul(
                pb2[0:32, 0:512], wu.ap()[0:32, 3072 + h * 32:3072 + (h + 1) * 32],
                cq.ap()[0:32, qc * 512:(qc + 1) * 512], start=True, stop=True),
                reads=[K("wu"), ("cq",)], writes=[K("ps", 4)])
            p.op("scalar", lambda e, h=h, qc=qc, pb2=pb2: e.activation(
                qrc.ap()[0:32, h, :], pb2[0:32, 0:512], AF.Identity,
                bias=bqr.ap()[0:32, h:h + 1], scale=1.0),
                reads=[K("ps", 4), K("bqr")], writes=[("qrc", h)])
        for h in range(H):
            for kt in range(16):
                pb = ctx.psum[3 + (kt % 2)]
                pk = K("ps", 3 + (kt % 2))
                p.op("tensor", lambda e, h=h, kt=kt, pb=pb: e.matmul(
                    pb[:, 0:512], kc.ap()[:, h, kt * 128:(kt + 1) * 128],
                    qcc.ap()[:, h, :], start=True, stop=False),
                    reads=[("kc", h), ("qcc", h)], writes=[pk])
                p.op("tensor", lambda e, h=h, kt=kt, pb=pb: e.matmul(
                    pb[:, 0:512], kr.ap()[0:32, kt * 128:(kt + 1) * 128],
                    qrc.ap()[0:32, h, :], start=False, stop=True),
                    reads=[K("kr"), ("qrc", h)], writes=[pk])
                p.op("scalar", lambda e, kt=kt, pb=pb: e.activation(
                    expS.ap()[:, kt, :], pb[:, 0:512], AF.Exp, bias=csh.ap()[:], scale=scale),
                    reads=[pk, K("csh")], writes=[("expS", kt)])
            pa = ctx.psum[5]
            pb2 = ctx.psum[6]
            for kt in range(16):
                p.op("tensor", lambda e, h=h, kt=kt, pa=pa: e.matmul(
                    pa[0:65, 0:512], vtm.ap()[:, kt, h, 0:65], expS.ap()[:, kt, :],
                    start=(kt == 0), stop=(kt == 15)),
                    reads=[("vtm", kt), ("expS", kt), K("vones")], writes=[K("ps", 5)])
            for kt in range(16):
                p.op("tensor", lambda e, h=h, kt=kt, pb2=pb2: e.matmul(
                    pb2[0:65, 0:512], vtm.ap()[:, kt, h, 65:130], expS.ap()[:, kt, :],
                    start=(kt == 0), stop=(kt == 15)),
                    reads=[("vtm", kt), ("expS", kt), K("vones")], writes=[K("ps", 6)])
            p.op("vector", lambda e, pa=pa: e.reciprocal(rden.ap()[:], pa[64:65, 0:512]),
                 reads=[K("ps", 5)], writes=[K("rden")])
            p.op("tensor", lambda e: e.matmul(
                ctx.psum[3][0:64, 0:512], ctx.ones1f[0:1, 0:64], rden.ap()[:],
                start=True, stop=True),
                reads=[K("rden"), K("ones1f")], writes=[K("ps", 3)])
            p.op("scalar", lambda e, pa=pa: e.copy(arow.ap()[:], pa[0:64, 0:512]),
                 reads=[K("ps", 5)], writes=[K("arow")])
            p.op("vector", lambda e, h=h: e.tensor_tensor(
                att.ap()[0:64, h, :], arow.ap()[:], ctx.psum[3][0:64, 0:512], OP.mult),
                reads=[K("arow"), K("ps", 3)], writes=[("att", h)])
            p.op("scalar", lambda e, pb2=pb2: e.copy(arow.ap()[:], pb2[0:64, 0:512]),
                 reads=[K("ps", 6)], writes=[K("arow")])
            p.op("vector", lambda e: e.tensor_tensor(
                anrm.ap()[:], arow.ap()[:], ctx.psum[3][0:64, 0:512], OP.mult),
                reads=[K("arow"), K("ps", 3)], writes=[K("anrm")])
            p.op("tensor", lambda e: e.matmul(
                ctx.psum[6][64:128, 0:512], ctx.idn[0:64, 0:64], anrm.ap()[:],
                start=True, stop=True, tile_position=(0, 64)),
                reads=[K("anrm"), K("idn")], writes=[K("ps", 6)])
            p.op("vector", lambda e, h=h: e.tensor_copy(
                att.ap()[64:128, h, :], ctx.psum[6][64:128, 0:512]),
                reads=[K("ps", 6)], writes=[("att", h)])
        for ge in range(G):
            wob = wo[ge % 2]
            wkk = ("wo", ge % 2)
            p.op("gpsimd", lambda e, ge=ge, wob=wob: e.dma_start(
                wob.ap()[:, :, :], dr["l3a_wo"].rearrange("(g p) n -> p g n", p=128)[:, :, ge * 128:(ge + 1) * 128]),
                reads=[wkk], writes=[wkk], dma=True)
            pb = ctx.psum[4]
            for g in range(G):
                p.op("tensor", lambda e, g=g, ge=ge, wob=wob, pb=pb: e.matmul(
                    pb[:, 0:512], wob.ap()[:, g, :], att.ap()[:, g, :],
                    start=(g == 0), stop=(g == G - 1)),
                    reads=[wkk, ("att", g)], writes=[K("ps", 4)])
            p.op("vector", lambda e, ge=ge, qc=qc, pb=pb: e.tensor_copy(
                acc.ap()[:, ge, qc * 512:(qc + 1) * 512], pb[:, 0:512]),
                reads=[K("ps", 4)], writes=[("acc", ge)])
    p.barrier()
    es2.close()
    p.barrier()
    ctx.x = ctx.xstack.enter_context(nc.sbuf_tensor(uname("x_fm"), [128, G, T], F32))
    xall2 = ctx.x.ap()[:, :, :].rearrange("p a b -> p (a b)")
    p.op("sync", lambda e, xall2=xall2: e.dma_start(xall2, ctx.xspill[:]),
         reads=[K("xspdram")], writes=XK, dma=True)
    for g in range(G):
        for tc in range(4):
            xs = ctx.x.ap()[:, g, tc * 512:(tc + 1) * 512]
            p.op("vector", lambda e, xs=xs, g=g, tc=tc: e.scalar_tensor_tensor(
                xs, acc.ap()[:, g, tc * 512:(tc + 1) * 512], beff.ap()[:, g:g + 1], xs,
                OP.add, OP.add),
                reads=[("x", g), ("acc", g), ("beffd", g)], writes=[("x", g)])

def emit_l2_attn(p, ctx, es, dr):
    nc = ctx.nc
    H = 8
    SEG = 512
    qfm = es.enter_context(nc.sbuf_tensor(uname("qfm"), [128, H, SEG], BF16, side="right"))
    kfm = es.enter_context(nc.sbuf_tensor(uname("kfm"), [128, H, SEG], BF16, side="right"))
    ktm = es.enter_context(nc.sbuf_tensor(uname("ktm"), [128, 4, 1024], BF16, side="right"))
    vtm = es.enter_context(nc.sbuf_tensor(uname("vtm"), [128, 4, 1024], BF16, side="right"))
    ofm = es.enter_context(nc.sbuf_tensor(uname("ofm"), [128, H, SEG], BF16, side="right"))
    wq = es.enter_context(nc.sbuf_tensor(uname("wq"), [128, G, 1024], BF16, side="right"))
    wk = es.enter_context(nc.sbuf_tensor(uname("wk"), [128, G, 1024], BF16, side="right"))
    wv = es.enter_context(nc.sbuf_tensor(uname("wv"), [128, G, 1024], BF16, side="right"))
    wo = [es.enter_context(nc.sbuf_tensor(uname(f"wo{i}"), [128, G, 128], BF16, side="right")) for i in range(2)]
    triz = es.enter_context(nc.sbuf_tensor(uname("triz"), [128, 1024], BF16, side="right"))
    tri64 = es.enter_context(nc.sbuf_tensor(uname("tri64"), [128, 64], BF16, side="right"))
    nwrep = es.enter_context(nc.sbuf_tensor(uname("nwrep"), [128, 128], F32, side="right"))
    nwcol = es.enter_context(nc.sbuf_tensor(uname("nwcol"), [128, 1], F32, side="right"))
    carry = es.enter_context(nc.sbuf_tensor(uname("carry"), [128, H], F32, side="right"))
    stf = es.enter_context(nc.sbuf_tensor(uname("stf"), [128, H, 128], F32, side="right"))
    stb = es.enter_context(nc.sbuf_tensor(uname("stb"), [128, H, 128], BF16, side="right"))
    qz = es.enter_context(nc.sbuf_tensor(uname("qz"), [128, 512], BF16, side="right"))
    denr = es.enter_context(nc.sbuf_tensor(uname("denr"), [1, 512], F32, side="right"))
    stsb = es.enter_context(nc.sbuf_tensor(uname("stsb"), [128, 64], BF16, side="right"))
    sqsc = es.enter_context(nc.sbuf_tensor(uname("sqsc"), [64, 128], F32, side="right"))
    rms = es.enter_context(nc.sbuf_tensor(uname("rms"), [64, 3], F32, side="right"))
    epsc = es.enter_context(nc.sbuf_tensor(uname("epsc"), [64, 1], F32, side="right"))
    onrm = es.enter_context(nc.sbuf_tensor(uname("onrm"), [64, 128], BF16, side="right"))
    for nm, b in (("l2a_wq", wq), ("l2a_wk", wk), ("l2a_wv", wv)):
        p.op("gpsimd", lambda e, nm=nm, b=b: e.dma_start(
            b.ap()[:, :, :], dr[nm].rearrange("(g p) n -> p g n", p=128)),
            reads=[], writes=[K(nm)], dma=True)
    p.op("sync", lambda e: e.dma_start(triz[:], dr["triz"][:]), reads=[], writes=[K("triz")], dma=True)
    p.op("sync", lambda e: e.dma_start(tri64[:], dr["triu64"][:]), reads=[], writes=[K("tri64")], dma=True)
    p.op("sync", lambda e: e.dma_start(nwcol[:], dr["l2a_nw"][:]), reads=[], writes=[K("nwcol")], dma=True)
    p.op("vector", lambda e: e.memset(carry.ap()[:], 0.0), reads=[], writes=[K("carry")])
    p.op("vector", lambda e: e.memset(stf.ap()[:, :, :], 0.0), reads=[], writes=[("stf", h) for h in range(H)])
    p.op("vector", lambda e: e.memset(stb.ap()[:, :, :], 0.0), reads=[], writes=[("stb", h) for h in range(H)])
    p.op("vector", lambda e: e.memset(epsc.ap()[:], 1e-5), reads=[], writes=[K("epsc")])
    p.op("tensor", lambda e: e.transpose(ctx.psum[6][0:1, 0:128], nwcol.ap()[:, 0:1], ctx.idnf[:]),
         reads=[K("nwcol"), K("idnf")], writes=[K("ps", 6)])
    p.op("vector", lambda e: e.tensor_copy(denr.ap()[:, 0:128], ctx.psum[6][0:1, 0:128]),
         reads=[K("ps", 6)], writes=[K("denr")])
    p.op("tensor", lambda e: e.matmul(ctx.psum[6][:, 0:128], ctx.ones1f[:], denr.ap()[:, 0:128],
                                      start=True, stop=True),
         reads=[K("denr"), K("ones1f")], writes=[K("ps", 6)])
    p.op("vector", lambda e: e.tensor_copy(nwrep.ap()[:], ctx.psum[6][:, 0:128]),
         reads=[K("ps", 6)], writes=[K("nwrep")])
    for seg in range(4):
        t0 = seg * SEG
        for h in range(H):
            for nm, wb, dst, dk in (("l2a_wq", wq, qfm, "qfm"), ("l2a_wk", wk, kfm, "kfm")):
                pb = ctx.psum[3]
                for g in range(G):
                    p.op("tensor", lambda e, g=g, h=h, t0=t0, wb=wb, pb=pb: e.matmul(
                        pb[:, 0:512], wb.ap()[:, g, h * 128:(h + 1) * 128],
                        ctx.xb.ap()[:, g, PADL + t0:PADL + t0 + SEG],
                        start=(g == 0), stop=(g == G - 1)),
                        reads=[K(nm), ("xb", g)], writes=[K("ps", 3)])
                p.op("scalar", lambda e, h=h, dst=dst, pb=pb: e.copy(
                    dst.ap()[:, h, :], pb[:, 0:512]),
                    reads=[K("ps", 3)], writes=[(dk, h)])
        for tt in range(4):
            for nm, wb, dst, dk in (("l2a_wk", wk, ktm, "ktm"), ("l2a_wv", wv, vtm, "vtm")):
                pb = ctx.psum[4]
                for g in range(G):
                    p.op("tensor", lambda e, g=g, tt=tt, t0=t0, wb=wb, pb=pb: e.matmul(
                        pb[:, 0:512], ctx.xb.ap()[:, g, PADL + t0 + tt * 128:PADL + t0 + (tt + 1) * 128],
                        wb.ap()[:, g, :].rearrange("p n -> p n") if False else wb.ap()[:, g, :],
                        start=(g == 0), stop=(g == G - 1)),
                        reads=[K(nm), ("xb", g)], writes=[K("ps", 4)]) if False else None
                for half in range(2):
                    for g in range(G):
                        p.op("tensor", lambda e, g=g, tt=tt, t0=t0, half=half, wb=wb, pb=pb: e.matmul(
                            pb[:, 0:512], ctx.xb.ap()[:, g, PADL + t0 + tt * 128:PADL + t0 + (tt + 1) * 128],
                            wb.ap()[:, g, half * 512:(half + 1) * 512],
                            start=(g == 0), stop=(g == G - 1)),
                            reads=[K(nm), ("xb", g)], writes=[K("ps", 4)])
                    p.op("vector", lambda e, tt=tt, half=half, dst=dst, pb=pb: e.tensor_copy(
                        dst.ap()[:, tt, half * 512:(half + 1) * 512], pb[:, 0:512]),
                        reads=[K("ps", 4)], writes=[(dk, tt)])
        for h in range(H):
            pb = ctx.psum[5]
            for tt in range(4):
                p.op("tensor", lambda e, h=h, tt=tt, pb=pb: e.matmul(
                    pb[:, 0:512], ktm.ap()[:, tt, h * 128:(h + 1) * 128],
                    triz.ap()[:, 512 - tt * 128:1024 - tt * 128],
                    start=(tt == 0), stop=(tt == 3)),
                    reads=[("ktm", tt), K("triz")], writes=[K("ps", 5)])
            p.op("vector", lambda e, h=h, pb=pb: e.tensor_scalar(
                pb[:, 0:512], pb[:, 0:512], carry.ap()[:, h:h + 1], None, OP.add),
                reads=[K("ps", 5), K("carry")], writes=[K("ps", 5)])
            p.op("vector", lambda e, h=h, pb=pb: e.tensor_copy(
                carry.ap()[:, h:h + 1], pb[:, 511:512]),
                reads=[K("ps", 5)], writes=[K("carry")])
            p.op("vector", lambda e, h=h, pb=pb: e.tensor_tensor(
                qz.ap()[:], qfm.ap()[:, h, :], pb[:, 0:512], OP.mult),
                reads=[K("ps", 5), ("qfm", h), K("qz")], writes=[K("qz")])
            p.op("tensor", lambda e, pb=pb: e.matmul(
                ctx.psum[6][0:1, 0:512], ctx.onescol[:], qz.ap()[:],
                start=True, stop=True),
                reads=[K("qz"), K("onescol")], writes=[K("ps", 6)])
            p.op("scalar", lambda e: e.activation(
                denr.ap()[:], ctx.psum[6][0:1, 0:512], AF.Abs, bias=0.0, scale=1.0),
                reads=[K("ps", 6), K("denr")], writes=[K("denr")])
            p.op("vector", lambda e: e.tensor_scalar(denr.ap()[:], denr.ap()[:], 1.0, None, OP.max),
                 reads=[K("denr")], writes=[K("denr")])
            p.op("vector", lambda e: e.reciprocal(denr.ap()[:], denr.ap()[:]),
                 reads=[K("denr")], writes=[K("denr")])
            p.op("tensor", lambda e: e.matmul(
                ctx.psum[6][:, 0:512], ctx.ones1f[:], denr.ap()[:], start=True, stop=True),
                reads=[K("denr"), K("ones1f")], writes=[K("ps", 6)])
            p.op("vector", lambda e, h=h: e.tensor_tensor(
                qfm.ap()[:, h, :], qfm.ap()[:, h, :], ctx.psum[6][:, 0:512], OP.mult),
                reads=[K("ps", 6), ("qfm", h)], writes=[("qfm", h)])
        for cc in range(8):
            tt, hh = cc // 2, (cc % 2) * 64
            for h in range(H):
                pio = ctx.psum[3]
                pst = ctx.psum[4]
                psc = ctx.psum[5]
                p.op("tensor", lambda e, h=h, cc=cc, pio=pio: e.matmul(
                    pio[0:64, 0:128], qfm.ap()[:, h, cc * 64:(cc + 1) * 64],
                    stb.ap()[:, h, :], start=True, stop=False),
                    reads=[("qfm", h), ("stb", h)], writes=[K("ps", 3)])
                tp = (0, 64) if hh == 64 else None
                p.op("tensor", lambda e, h=h, cc=cc, hh=hh, psc=psc, tp=tp: e.matmul(
                    psc[hh:hh + 64, 0:64], kfm.ap()[:, h, cc * 64:(cc + 1) * 64],
                    qfm.ap()[:, h, cc * 64:(cc + 1) * 64], start=True, stop=True,
                    tile_position=tp),
                    reads=[("kfm", h), ("qfm", h)], writes=[K("ps", 5)])
                p.op("vector", lambda e, hh=hh, psc=psc: e.tensor_tensor(
                    stsb.ap()[hh:hh + 64, :], psc[hh:hh + 64, 0:64], tri64.ap()[hh:hh + 64, :], OP.mult),
                    reads=[K("ps", 5), K("tri64"), K("stsb")], writes=[K("stsb")])
                p.op("tensor", lambda e, h=h, tt=tt, hh=hh, pio=pio: e.matmul(
                    pio[0:64, 0:128], stsb.ap()[hh:hh + 64, :],
                    vtm.ap()[hh:hh + 64, tt, h * 128:(h + 1) * 128],
                    start=False, stop=True),
                    reads=[K("stsb"), ("vtm", tt)], writes=[K("ps", 3)])
                p.op("tensor", lambda e, h=h, tt=tt, hh=hh, pst=pst: e.matmul(
                    pst[:, 0:128], ktm.ap()[hh:hh + 64, tt, h * 128:(h + 1) * 128],
                    vtm.ap()[hh:hh + 64, tt, h * 128:(h + 1) * 128],
                    start=True, stop=True),
                    reads=[("ktm", tt), ("vtm", tt)], writes=[K("ps", 4)])
                p.op("scalar", lambda e, pio=pio: e.activation(
                    sqsc.ap()[:], pio[0:64, 0:128], AF.Square, bias=0.0, scale=1.0,
                    accum_out=rms.ap()[:, 0:1]),
                    reads=[K("ps", 3), K("rms"), K("sqsc")], writes=[K("rms"), K("sqsc")])
                p.op("scalar", lambda e: e.activation(
                    rms.ap()[:, 1:2], rms.ap()[:, 0:1], AF.Sqrt, bias=epsc.ap()[:, 0:1], scale=1.0 / 128),
                    reads=[K("rms"), K("epsc")], writes=[K("rms")])
                p.op("vector", lambda e: e.reciprocal(rms.ap()[:, 2:3], rms.ap()[:, 1:2]),
                     reads=[K("rms")], writes=[K("rms")])
                p.op("vector", lambda e, pio=pio: e.scalar_tensor_tensor(
                    onrm.ap()[:], pio[0:64, 0:128], rms.ap()[:, 2:3], nwrep.ap()[0:64, :],
                    OP.mult, OP.mult),
                    reads=[K("ps", 3), K("rms"), K("nwrep"), K("onrm")], writes=[K("onrm")])
                p.op("tensor", lambda e, pst=pst: e.transpose(
                    ctx.psT[:, 0:64], onrm.ap()[:], ctx.idn[0:64, 0:64]),
                    reads=[K("onrm"), K("idn")], writes=[K("psT")])
                p.op("vector", lambda e, h=h, cc=cc: e.tensor_copy(
                    ofm.ap()[:, h, cc * 64:(cc + 1) * 64], ctx.psT[:, 0:64]),
                    reads=[K("psT")], writes=[("ofm", h)])
                p.op("vector", lambda e, h=h, pst=pst: e.tensor_tensor(
                    stf.ap()[:, h, :], stf.ap()[:, h, :], pst[:, 0:128], OP.add),
                    reads=[K("ps", 4), ("stf", h)], writes=[("stf", h)])
                p.op("scalar", lambda e, h=h: e.copy(stb.ap()[:, h, :], stf.ap()[:, h, :]),
                     reads=[("stf", h)], writes=[("stb", h)])
        for dout in range(G):
            wob = wo[dout % 2]
            wkk = ("wo", dout % 2)
            p.op("gpsimd", lambda e, dout=dout, wob=wob: e.dma_start(
                wob.ap()[:, :, :], dr["l2a_wo"].rearrange("(g p) n -> p g n", p=128)[:, :, dout * 128:(dout + 1) * 128]),
                reads=[wkk], writes=[wkk], dma=True)
            pb = ctx.psum[6]
            for h in range(G):
                p.op("tensor", lambda e, h=h, dout=dout, wob=wob, pb=pb: e.matmul(
                    pb[:, 0:512], wob.ap()[:, h, :], ofm.ap()[:, h, :],
                    start=(h == 0), stop=(h == G - 1)),
                    reads=[wkk, ("ofm", h)], writes=[K("ps", 6)])
            xs = ctx.x.ap()[:, dout, t0:t0 + SEG]
            p.op("vector", lambda e, xs=xs, dout=dout, pb=pb: e.tensor_tensor(
                xs, xs, pb[:, 0:512], OP.add),
                reads=[K("ps", 6), ("x", dout)], writes=[("x", dout)])

def emit_l2_ffn(p, ctx, es, dr):
    nc = ctx.nc
    E = 8
    xsp = ctx.xspill
    acc = es.enter_context(nc.sbuf_tensor(uname("acc"), [128, G, T], BF16, side="right"))
    rwb = es.enter_context(nc.sbuf_tensor(uname("rwb"), [128, G, 8], BF16, side="right"))
    rbias = es.enter_context(nc.sbuf_tensor(uname("rbias"), [8, 1], F32, side="right"))
    selrep = es.enter_context(nc.sbuf_tensor(uname("selrep"), [8, 8 * 128], F32, side="right"))
    lfm = es.enter_context(nc.sbuf_tensor(uname("lfm"), [8, T], F32, side="right"))
    ltm = es.enter_context(nc.sbuf_tensor(uname("ltm"), [128, 16, 8], F32, side="right"))
    cfm = es.enter_context(nc.sbuf_tensor(uname("cfm"), [8, T], F32, side="right"))
    r1 = es.enter_context(nc.sbuf_tensor(uname("r1"), [128, 2], F32, side="right"))
    p.op("gpsimd", lambda e: e.dma_start(rwb.ap()[:, :, :], dr["l2f_rw"].rearrange("(g p) n -> p g n", p=128)),
         reads=[], writes=[K("rwb")], dma=True)
    p.op("sync", lambda e: e.dma_start(rbias[:], dr["l2f_rb"][:]), reads=[], writes=[K("rbias")], dma=True)
    p.op("sync", lambda e: e.dma_start(selrep[:], dr["selrep8"][:]), reads=[], writes=[K("selrep8")], dma=True)
    for tc in range(4):
        pb = ctx.psum[3]
        for g in range(G):
            p.op("tensor", lambda e, g=g, tc=tc, pb=pb: e.matmul(
                pb[0:8, 0:512], rwb.ap()[:, g, :],
                ctx.xb.ap()[:, g, PADL + tc * 512:PADL + (tc + 1) * 512],
                start=(g == 0), stop=(g == G - 1)),
                reads=[K("rwb"), ("xb", g)], writes=[K("ps", 3)])
        p.op("scalar", lambda e, tc=tc, pb=pb: e.activation(
            lfm.ap()[0:8, tc * 512:(tc + 1) * 512], pb[0:8, 0:512],
            AF.Identity, bias=rbias.ap()[0:8, 0:1], scale=1.0),
            reads=[K("ps", 3), K("rbias")], writes=[("lfm", tc)])
    for tt in range(16):
        pb = ctx.psum[4]
        p.op("tensor", lambda e, tt=tt, pb=pb: e.transpose(
            pb[:, 0:8], lfm.ap()[0:8, tt * 128:(tt + 1) * 128], ctx.idnf[0:8, 0:8]),
            reads=[("lfm", tt // 4), K("idnf")], writes=[K("ps", 4)])
        p.op("vector", lambda e, tt=tt, pb=pb: e.tensor_copy(ltm.ap()[:, tt, :], pb[:, 0:8]),
             reads=[K("ps", 4)], writes=[("ltm", tt)])
    for tt in range(16):
        l = ltm.ap()[:, tt, :]
        m1 = r1.ap()[:, 0:1]
        p.op("vector", lambda e, l=l, m1=m1: e.reduce_max(m1, l, axis=AX.X),
             reads=[("ltm", tt), K("r1")], writes=[K("r1")])
        p.op("vector", lambda e, l=l, m1=m1: e.tensor_scalar(l, l, m1, None, OP.is_equal),
             reads=[("ltm", tt), K("r1")], writes=[("ltm", tt)])
    for tt in range(16):
        pb = ctx.psum[4]
        p.op("tensor", lambda e, tt=tt, pb=pb: e.transpose(
            pb[0:8, 0:128], ltm.ap()[:, tt, :], ctx.idnf[:]),
            reads=[("ltm", tt), K("idnf")], writes=[K("ps", 4)])
        p.op("vector", lambda e, tt=tt, pb=pb: e.tensor_copy(
            cfm.ap()[0:8, tt * 128:(tt + 1) * 128], pb[0:8, 0:128]),
            reads=[K("ps", 4)], writes=[("cfm", tt // 4)])
    xall = ctx.x.ap()[:, :, :].rearrange("p a b -> p (a b)")
    p.op("sync", lambda e, xall=xall: e.dma_start(xsp[:], xall),
         reads=XK, writes=[K("xspdram")], dma=True)
    p.barrier()
    ctx.xstack.close()
    es2 = ExitStack()
    try:
        act = es2.enter_context(nc.sbuf_tensor(uname("act"), [128, 32, 1024], BF16, side="right"))
        wa = [es2.enter_context(nc.sbuf_tensor(uname(f"wa{i}"), [128, G, 512], BF16, side="right")) for i in range(2)]
        wg = [es2.enter_context(nc.sbuf_tensor(uname(f"wg{i}"), [128, G, 512], BF16, side="right")) for i in range(2)]
        w2b = [es2.enter_context(nc.sbuf_tensor(uname(f"w2b{i}"), [128, 32, 128], BF16, side="right")) for i in range(2)]
        b1c = es2.enter_context(nc.sbuf_tensor(uname("b1c"), [128, 8 * 64], F32, side="right"))
        b2c = es2.enter_context(nc.sbuf_tensor(uname("b2c"), [128, 8 * G], F32, side="right"))
        crep = es2.enter_context(nc.sbuf_tensor(uname("crep"), [128, 1024], F32, side="right"))
        sig = es2.enter_context(nc.sbuf_tensor(uname("sig"), [128, 512], F32, side="right"))
        tbf = es2.enter_context(nc.sbuf_tensor(uname("tbf"), [128, 512], BF16, side="right"))
        p.op("sync", lambda e: e.dma_start(b1c[:], dr["l2f_b1"][:]), reads=[], writes=[K("b1c")], dma=True)
        p.op("sync", lambda e: e.dma_start(b2c[:], dr["l2f_b2"][:]), reads=[], writes=[K("b2c")], dma=True)
        p.op("vector", lambda e: e.memset(acc.ap()[:, :, :], 0.0), reads=[], writes=[("acc", g) for g in range(G)])
        w1d = dr["l2f_w1"].rearrange("e (g p) f -> e p g f", p=128)
        w2d = dr["l2f_w2"].rearrange("e (c p) f -> e p c f", p=128)
        for th in range(2):
            for ex in range(E):
                bi = ex % 2
                for q in range(2):
                    pb = ctx.psum[5]
                    p.op("tensor", lambda e, ex=ex, th=th, q=q, pb=pb: e.matmul(
                        pb[:, 0:512], selrep.ap()[:, ex * 128:(ex + 1) * 128],
                        cfm.ap()[0:8, th * 1024 + q * 512:th * 1024 + (q + 1) * 512],
                        start=True, stop=True),
                        reads=[("cfm", th * 2 + q), K("selrep8")], writes=[K("ps", 5)])
                    p.op("vector", lambda e, q=q, pb=pb: e.tensor_copy(
                        crep.ap()[:, q * 512:(q + 1) * 512], pb[:, 0:512]),
                        reads=[K("ps", 5)], writes=[("crep", q)])
                for sl in range(8):
                    wbuf = wa[sl % 2] if sl % 2 == 0 else wa[1]
                    wbuf = wa[sl % 2]
                    gbuf = wg[sl % 2]
                    wk = ("wa", sl % 2); gk = ("wg", sl % 2)
                    p.op("gpsimd", lambda e, ex=ex, sl=sl, wbuf=wbuf: e.dma_start(
                        wbuf.ap()[:, :, :], w1d[ex, :, :, sl * 512:(sl + 1) * 512]),
                        reads=[], writes=[wk], dma=True)
                    p.op("gpsimd", lambda e, ex=ex, sl=sl, gbuf=gbuf: e.dma_start(
                        gbuf.ap()[:, :, :], w1d[ex, :, :, 4096 + sl * 512:4096 + (sl + 1) * 512]),
                        reads=[], writes=[gk], dma=True)
                    for fi in range(4):
                        fc = sl * 4 + fi
                        b1a = b1c.ap()[:, ex * 64 + fc:ex * 64 + fc + 1]
                        b1g = b1c.ap()[:, ex * 64 + 32 + fc:ex * 64 + 32 + fc + 1]
                        for q in range(2):
                            pa = ctx.psum[3]; pg = ctx.psum[4]
                            for g in range(G):
                                p.op("tensor", lambda e, g=g, fi=fi, th=th, q=q, wbuf=wbuf, pa=pa: e.matmul(
                                    pa[:, 0:512], wbuf.ap()[:, g, fi * 128:(fi + 1) * 128],
                                    ctx.xb.ap()[:, g, PADL + th * 1024 + q * 512:PADL + th * 1024 + (q + 1) * 512],
                                    start=(g == 0), stop=(g == G - 1)),
                                    reads=[wk, ("xb", g)], writes=[K("ps", 3)])
                            for g in range(G):
                                p.op("tensor", lambda e, g=g, fi=fi, th=th, q=q, gbuf=gbuf, pg=pg: e.matmul(
                                    pg[:, 0:512], gbuf.ap()[:, g, fi * 128:(fi + 1) * 128],
                                    ctx.xb.ap()[:, g, PADL + th * 1024 + q * 512:PADL + th * 1024 + (q + 1) * 512],
                                    start=(g == 0), stop=(g == G - 1)),
                                    reads=[gk, ("xb", g)], writes=[K("ps", 4)])
                            p.op("scalar", lambda e, pg=pg, b1g=b1g: e.activation(
                                sig.ap()[:], pg[:, 0:512], AF.Sigmoid, bias=b1g, scale=1.0),
                                reads=[K("ps", 4), K("b1c"), K("sig")], writes=[K("sig")])
                            p.op("vector", lambda e, fc=fc, q=q, pa=pa, b1a=b1a: e.scalar_tensor_tensor(
                                act.ap()[:, fc, q * 512:(q + 1) * 512], pa[:, 0:512], b1a, sig.ap()[:],
                                OP.add, OP.mult),
                                reads=[K("ps", 3), K("b1c"), K("sig")], writes=[("act", fc)])
                for dout in range(G):
                    wk2 = ("w2b", dout % 2)
                    p.op("gpsimd", lambda e, ex=ex, dout=dout: e.dma_start(
                        w2b[dout % 2].ap()[:, :, :], w2d[ex, :, :, dout * 128:(dout + 1) * 128]),
                        reads=[], writes=[wk2], dma=True)
                    b2col = b2c.ap()[:, ex * G + dout:ex * G + dout + 1]
                    for q in range(2):
                        pb = ctx.psum[5 if q == 0 else 6]
                        pk = K("ps", 5 if q == 0 else 6)
                        for fc in range(32):
                            p.op("tensor", lambda e, fc=fc, dout=dout, q=q, pb=pb: e.matmul(
                                pb[:, 0:512], w2b[dout % 2].ap()[:, fc, :],
                                act.ap()[:, fc, q * 512:(q + 1) * 512],
                                start=(fc == 0), stop=(fc == 31)),
                                reads=[wk2, ("act", fc)], writes=[pk])
                        p.op("vector", lambda e, q=q, pb=pb, b2col=b2col: e.scalar_tensor_tensor(
                            tbf.ap()[:], pb[:, 0:512], b2col, crep.ap()[:, q * 512:(q + 1) * 512],
                            OP.add, OP.mult),
                            reads=[pk, K("b2c"), ("crep", q), K("tbf")], writes=[K("tbf")])
                        accs = acc.ap()[:, dout, th * 1024 + q * 512:th * 1024 + (q + 1) * 512]
                        p.op("vector", lambda e, accs=accs, dout=dout: e.tensor_tensor(
                            accs, accs, tbf.ap()[:], OP.add),
                            reads=[K("tbf"), ("acc", dout)], writes=[("acc", dout)])
        p.barrier()
    finally:
        es2.close()
    p.barrier()
    ctx.x = ctx.xstack.enter_context(nc.sbuf_tensor(uname("x_fm"), [128, G, T], F32))
    xall2 = ctx.x.ap()[:, :, :].rearrange("p a b -> p (a b)")
    p.op("sync", lambda e, xall2=xall2: e.dma_start(xall2, xsp[:]),
         reads=[K("xspdram")], writes=XK, dma=True)
    for g in range(G):
        for tc in range(4):
            xs = ctx.x.ap()[:, g, tc * 512:(tc + 1) * 512]
            p.op("vector", lambda e, xs=xs, g=g, tc=tc: e.tensor_tensor(
                xs, xs, acc.ap()[:, g, tc * 512:(tc + 1) * 512], OP.add),
                reads=[("x", g), ("acc", g)], writes=[("x", g)])

def emit_l1_ffn(p, ctx, es, dr):
    nc = ctx.nc
    wr = es.enter_context(nc.sbuf_tensor(uname("wr"), [128, G, 1024], BF16, side="right"))
    wkb = es.enter_context(nc.sbuf_tensor(uname("wkb"), [128, G, 1024], BF16, side="right"))
    wvb = es.enter_context(nc.sbuf_tensor(uname("wvb"), [128, G, 1024], BF16, side="right"))
    mus = es.enter_context(nc.sbuf_tensor(uname("mus"), [128, 2 * G], F32, side="right"))
    oms = es.enter_context(nc.sbuf_tensor(uname("oms"), [128, 2 * G], F32, side="right"))
    xrk = es.enter_context(nc.sbuf_tensor(uname("xrk"), [128, G, 512], BF16, side="right"))
    rsb = es.enter_context(nc.sbuf_tensor(uname("rsb"), [128, G, 512], BF16, side="right"))
    kact = es.enter_context(nc.sbuf_tensor(uname("kact"), [128, G, 512], BF16, side="right"))
    tb = es.enter_context(nc.sbuf_tensor(uname("tb"), [128, 512], BF16, side="right"))
    tf = es.enter_context(nc.sbuf_tensor(uname("tf"), [128, 512], F32, side="right"))
    for nm, buf in (("l1f_wr", wr), ("l1f_wk", wkb), ("l1f_wv", wvb)):
        p.op("gpsimd", lambda e, nm=nm, buf=buf: e.dma_start(
            buf.ap()[:, :, :], dr[nm].rearrange("(c p) n -> p c n", p=128)),
            reads=[], writes=[K(nm)], dma=True)
    p.op("sync", lambda e: e.dma_start(mus[:], dr["l1f_mu"][:]), reads=[], writes=[K("mus")], dma=True)
    p.op("vector", lambda e: e.tensor_scalar(oms.ap()[:], mus.ap()[:], -1.0, 1.0, OP.mult, OP.add),
         reads=[K("mus")], writes=[K("oms")])
    for tc in range(4):
        def shift_mix(which):
            for g in range(G):
                mu = mus.ap()[:, 2 * g + which:2 * g + which + 1]
                om = oms.ap()[:, 2 * g + which:2 * g + which + 1]
                cur = ctx.xb.ap()[:, g, PADL + tc * 512:PADL + (tc + 1) * 512]
                prv = ctx.xb.ap()[:, g, PADL - 1 + tc * 512:PADL - 1 + (tc + 1) * 512]
                dst = xrk.ap()[:, g, :]
                p.op("vector", lambda e, cur=cur, mu=mu: e.tensor_scalar(
                    tb.ap()[:], cur, mu, None, OP.mult),
                    reads=[("xb", g), K("mus"), K("tbb")], writes=[K("tbb")])
                p.op("vector", lambda e, prv=prv, om=om, dst=dst, g=g: e.scalar_tensor_tensor(
                    dst, prv, om, tb.ap()[:], OP.mult, OP.add),
                    reads=[("xb", g), K("oms"), K("tbb")], writes=[("xrk", g)])
        shift_mix(0)
        for ge in range(G):
            pb = ctx.psum[3 + (ge % 2)]
            pk = K("ps", 3 + (ge % 2))
            for g in range(G):
                p.op("tensor", lambda e, g=g, ge=ge, pb=pb: e.matmul(
                    pb[:, 0:512], wr.ap()[:, g, ge * 128:(ge + 1) * 128],
                    xrk.ap()[:, g, :], start=(g == 0), stop=(g == G - 1)),
                    reads=[K("l1f_wr"), ("xrk", g)], writes=[pk])
            p.op("scalar", lambda e, ge=ge, pb=pb: e.activation(
                rsb.ap()[:, ge, :], pb[:, 0:512], AF.Sigmoid, bias=0.0, scale=1.0),
                reads=[pk], writes=[("rsb", ge)])
        shift_mix(1)
        for ge in range(G):
            pb = ctx.psum[3 + (ge % 2)]
            pk = K("ps", 3 + (ge % 2))
            for g in range(G):
                p.op("tensor", lambda e, g=g, ge=ge, pb=pb: e.matmul(
                    pb[:, 0:512], wkb.ap()[:, g, ge * 128:(ge + 1) * 128],
                    xrk.ap()[:, g, :], start=(g == 0), stop=(g == G - 1)),
                    reads=[K("l1f_wk"), ("xrk", g)], writes=[pk])
            p.op("vector", lambda e, pb=pb: e.tensor_scalar(
                tb.ap()[:], pb[:, 0:512], 0.0, None, OP.max),
                reads=[pk, K("tbb")], writes=[K("tbb")])
            p.op("scalar", lambda e, ge=ge: e.activation(
                kact.ap()[:, ge, :], tb.ap()[:], AF.Square, bias=0.0, scale=1.0),
                reads=[K("tbb")], writes=[("kact", ge)])
        for ge in range(G):
            pb = ctx.psum[3 + (ge % 2)]
            pk = K("ps", 3 + (ge % 2))
            for g in range(G):
                p.op("tensor", lambda e, g=g, ge=ge, pb=pb: e.matmul(
                    pb[:, 0:512], wvb.ap()[:, g, ge * 128:(ge + 1) * 128],
                    kact.ap()[:, g, :], start=(g == 0), stop=(g == G - 1)),
                    reads=[K("l1f_wv"), ("kact", g)], writes=[pk])
            p.op("vector", lambda e, ge=ge, pb=pb: e.tensor_tensor(
                tf.ap()[:], rsb.ap()[:, ge, :], pb[:, 0:512], OP.mult),
                reads=[pk, ("rsb", ge), K("tff")], writes=[K("tff")])
            xs = ctx.x.ap()[:, ge, tc * 512:(tc + 1) * 512]
            p.op("vector", lambda e, xs=xs: e.tensor_tensor(xs, xs, tf.ap()[:], OP.add),
                 reads=[K("tff"), ("x", ge)], writes=[("x", ge)])

BLOCK_WEIGHTS = {
    "l0_attn": ["l0_wqkv", "l0_bqk", "l0_bv", "l0_wo", "l0_bo", "maskL0"],
    "l1_attn": ["l1_wqkv", "l1_bqk", "l1_bv", "l1_wo", "l1_bo", "maskL1"],
    "l1_ffn": ["l1f_wr", "l1f_wk", "l1f_wv", "l1f_mu"],
    "l0_ffn": ["l0f_rw", "l0f_rb", "l0f_gate", "l0f_up", "l0f_down", "l0f_conv", "l0f_sup", "l0f_sdn"],
    "l2_ffn": ["l2f_rw", "l2f_rb", "l2f_w1", "l2f_b1", "l2f_w2", "l2f_b2"],
    "l2_attn": ["l2a_wq", "l2a_wk", "l2a_wv", "l2a_wo", "l2a_nw", "triz", "triu64"],
    "l3_attn": ["l3a_wdq", "l3a_wdkv", "l3a_wuq", "l3a_wuk", "l3a_wuv", "l3a_wqr", "l3a_wkr",
                "l3a_wo", "l3a_bd", "l3a_buqk", "l3a_bqr", "l3a_bkr", "l3a_bo", "l3a_bvp"],
    "l3_ffn": ["l3f_qkvw", "l3f_projw", "l3f_rw", "l3f_w1", "l3f_w2", "l3f_ln",
               "l3f_mask", "l3f_selcol", "selrep8"],
}
CONST_NAMES = ["idn", "idnf", "ones1", "ones1f", "onescol", "selrep16", "selrep8", "triz", "triu64"]

DBG = []
def dbg_dump(p, ctx, name, ap, shape, dt, reads):
    import os as _o
    if not _o.environ.get("KDBG"):
        return
    t = ctx.nc.dram_tensor("dbg_" + name, list(shape), dt, kind="ExternalOutput").ap()
    p.op("sync", lambda e, ap=ap, t=t: e.dma_start(t[:], ap), reads=reads, writes=[], dma=True)

def build_launch(blocks, arrs):
    nc = bass.Bass("TRN2", target_bir_lowering=False, debug=False)
    dr = {}
    x_in = nc.dram_tensor("x_in", [T, D], F32, kind="ExternalInput").ap()
    x_out = nc.dram_tensor("x_out", [T, D], F32, kind="ExternalOutput").ap()
    names = list(CONST_NAMES)
    for b in blocks:
        names += BLOCK_WEIGHTS[b]
    for name in sorted(set(names)):
        v = arrs[name][0] if name in PER_CORE else arrs[name]
        dt = BF16 if v.dtype == ml_dtypes.bfloat16 else F32
        dr[name] = nc.dram_tensor(name, list(v.shape), dt, kind="ExternalInput").ap()
    es = ExitStack()
    with es:
        ctx = Ctx()
        ctx.nc = nc
        sb = lambda name, shape, dt: es.enter_context(nc.sbuf_tensor(uname(name), shape, dt))
        ps = lambda name, shape, dt: es.enter_context(nc.psum_tensor(name, shape, dt))
        ctx.xspill = nc.dram_tensor("xspill", [128, G * T], F32, kind="Internal").ap()
        if "l3_ffn" in blocks:
            ctx.cc_kv_in_t = nc.dram_tensor("cc_kv_in", [T, 512], BF16)
            ctx.cc_kv_out_t = nc.dram_tensor("cc_kv_out", [NC * T, 512], BF16)
            ctx.cc_ct_in_t = nc.dram_tensor("cc_ct_in", [8, 1], F32)
            ctx.cc_ct_out_t = nc.dram_tensor("cc_ct_out", [NC * 8, 1], F32)
            ctx.cc_kv_in = ctx.cc_kv_in_t.ap()
            ctx.cc_kv_out = ctx.cc_kv_out_t.ap()
            ctx.cc_ct_in = ctx.cc_ct_in_t.ap()
            ctx.cc_ct_out = ctx.cc_ct_out_t.ap()
        ctx.xb = sb("xb", [128, G, PADL + T + 4], BF16)
        ctx.idn = sb("sb_idn", [128, 128], BF16)
        ctx.idnf = sb("sb_idnf", [128, 128], F32)
        ctx.ones1 = sb("sb_ones1", [1, 128], BF16)
        ctx.ones1f = sb("sb_ones1f", [1, 128], F32)
        ctx.onescol = sb("sb_onescol", [128, 1], BF16)
        ctx.psum = [ps(f"ps{i}", [128, 512], F32) for i in range(7)]
        ctx.psT = ps("psT", [128, 512], BF16)
        ctx.xstack = ExitStack()
        ctx.x = ctx.xstack.enter_context(nc.sbuf_tensor(uname("sx_fm"), [128, G, T], F32))
        p = Prog()
        emit_load_consts(p, ctx, dr)
        with phase(p, es) as ph:
            b0 = ph.enter_context(nc.sbuf_tensor(uname("ldx0"), [128, 1024], F32))
            b1 = ph.enter_context(nc.sbuf_tensor(uname("ldx1"), [128, 1024], F32))
            emit_load_x(p, ctx, b0, b1, x_in)
        for b in blocks:
            if b == "l0_attn":
                with phase(p, es) as ph:
                    emit_attn_16h(p, ctx, ph, dr, "l0")
                with phase(p, es) as ph:
                    emit_postln(p, ctx, ph)
            elif b == "l1_attn":
                with phase(p, es) as ph:
                    emit_attn_16h(p, ctx, ph, dr, "l1")
                with phase(p, es) as ph:
                    emit_postln(p, ctx, ph)
            elif b == "l3_ffn":
                with phase(p, es) as ph:
                    emit_l3_ffn(p, ctx, ph, dr, None)
                with phase(p, es) as ph:
                    emit_postln(p, ctx, ph)
            elif b == "l3_attn":
                with phase(p, es) as ph:
                    emit_l3_attn(p, ctx, ph, dr)
                with phase(p, es) as ph:
                    emit_postln(p, ctx, ph)
            elif b == "l2_attn":
                with phase(p, es) as ph:
                    emit_l2_attn(p, ctx, ph, dr)
                with phase(p, es) as ph:
                    emit_postln(p, ctx, ph)
            elif b == "l2_ffn":
                with phase(p, es) as ph:
                    emit_l2_ffn(p, ctx, ph, dr)
                with phase(p, es) as ph:
                    emit_postln(p, ctx, ph)
            elif b == "l0_ffn":
                with phase(p, es) as ph:
                    emit_l0_ffn(p, ctx, ph, dr)
                with phase(p, es) as ph:
                    emit_postln(p, ctx, ph)
            elif b == "l1_ffn":
                with phase(p, es) as ph:
                    emit_l1_ffn(p, ctx, ph, dr)
                with phase(p, es) as ph:
                    emit_postln(p, ctx, ph)
            else:
                raise ValueError(b)
        with phase(p, es) as ph:
            b0 = ph.enter_context(nc.sbuf_tensor(uname("stx0"), [128, 1024], F32))
            b1 = ph.enter_context(nc.sbuf_tensor(uname("stx1"), [128, 1024], F32))
            emit_store_x(p, ctx, b0, b1, x_out)
        p.finalize_and_emit(nc, es)
        ctx.xstack.close()
    return nc

PER_CORE = {"l3f_mask", "l3f_selcol"}

def per_core_arrays():
    m = np.zeros((NC, 1, 8), dtype=np.float32)
    s = np.zeros((NC, 8, 1), dtype=np.float32)
    for c in range(NC):
        m[c, 0, :] = [0.0 if sp <= c else -3.0e38 for sp in range(8)]
        s[c, :, 0] = [1.0 if cp < c else 0.0 for cp in range(8)]
    return {"l3f_mask": m, "l3f_selcol": s}

def run_blocks(blocks, x, arrs):
    nc = build_launch(blocks, arrs)
    in_maps = []
    names = set(CONST_NAMES)
    for b in blocks:
        names.update(BLOCK_WEIGHTS[b])
    for c in range(NC):
        m = {"x_in": np.ascontiguousarray(x[c])}
        for n in names:
            m[n] = arrs[n][c] if n in PER_CORE else arrs[n]
        in_maps.append(m)
    res = run_bass_kernel_spmd(nc, in_maps, core_ids=list(range(NC)))
    return np.stack([r["x_out"] for r in res.results], axis=0)

ALL_BLOCKS = ["l0_attn", "l0_ffn", "l1_attn", "l1_ffn", "l2_attn", "l2_ffn", "l3_attn", "l3_ffn"]

def kernel(**inputs):
    arrs = host_constants()
    arrs.update(prep_weights(inputs))
    arrs.update(per_core_arrays())
    x = np.asarray(inputs["L0_attn_x"])
    x = run_blocks(ALL_BLOCKS, x, arrs)
    return x
